# version 06
# baseline (speedup 1.0000x reference)
import sys
import numpy as np

sys.path.insert(0, "/opt/trn_rl_repo")

import concourse.bass as bass
import concourse.mybir as mybir
from concourse.bass_utils import run_bass_kernel_spmd

F32 = mybir.dt.float32
BF16 = mybir.dt.bfloat16
AF = mybir.ActivationFunctionType
ALU = mybir.AluOpType
AX = mybir.AxisListType

NCORE = 8
T = 2048
D = 1024
DI = 2048
NST = 16
TC = 512
NCH = T // TC
NOLN = False

ENGS = ("pe", "act", "dve", "pool", "sync")
REDIR = True


class Action:
    __slots__ = ("eng", "fn", "marked", "is_dma", "label", "waited")

    def __init__(self, eng, fn, is_dma=False, label=""):
        self.eng = eng
        self.fn = fn
        self.marked = False
        self.is_dma = is_dma
        self.label = label
        self.waited = False


class Ctx:
    def __init__(self, nc):
        self.nc = nc
        self.acts = {e: [] for e in ENGS}
        self._cm = []
        self._marks = []
        self.last_act = {e: None for e in ENGS}

    def push(self):
        self._marks.append(len(self._cm))

    def pop(self):
        m = self._marks.pop()
        for cm in reversed(self._cm[m:]):
            cm.__exit__(None, None, None)
        del self._cm[m:]

    def sb(self, name, shape, dtype):
        cm = self.nc.sbuf_tensor(name, list(shape), dtype)
        t = cm.__enter__()
        self._cm.append(cm)
        return t

    def ps(self, name, shape, dtype=F32):
        cm = self.nc.psum_tensor(name, list(shape), dtype)
        t = cm.__enter__()
        self._cm.append(cm)
        return t

    def on(self, eng, fn, is_dma=False, label=""):
        a = Action(eng, fn, is_dma, label)
        self.acts[eng].append(a)
        self.last_act[eng] = a
        return a

    def full_barrier(self, engs=ENGS):
        for e in engs:
            for o in engs:
                if o != e and self.last_act.get(o) is not None:
                    self.need(e, self.last_act[o])

    def dma(self, eng, out, in_, **kw):
        return self.on(eng,
                       lambda E, out=out, in_=in_, kw=kw: E.dma_start(out=out, in_=in_, **kw),
                       is_dma=True)

    def need(self, eng, *producers):
        for p in producers:
            if p is None or (isinstance(p, Action) and p.eng == eng):
                continue
            if isinstance(p, (list, tuple)):
                self.need(eng, *p)
                continue
            p.marked = True
            p.waited = True
            self.acts[eng].append(("wait", p))

    def barrier(self, producers, engs=ENGS):
        for e in engs:
            self.need(e, *producers)

    def build(self):
        nc = self.nc
        sems = {}
        for e in ENGS:
            cm = nc.semaphore(f"sem_{e}")
            sems[e] = cm.__enter__()
            self._cm.append(cm)
        val = {}
        for e in ENGS:
            c = 0
            for a in self.acts[e]:
                if isinstance(a, Action) and a.is_dma:
                    a.marked = True
                if isinstance(a, Action) and a.marked:
                    c += 16 if a.is_dma else 1
                    val[id(a)] = c
        sync_total = val.get(id(next(
            (a for a in reversed(self.acts["sync"]) if isinstance(a, Action)), None)), 0)

        def body(E, ename):
            E.sem_clear(sems[ename])
            waited = {}
            for a in self.acts[ename]:
                if isinstance(a, tuple):
                    p = a[1]
                    v = val[id(p)]
                    if waited.get(p.eng, 0) >= v:
                        continue
                    waited[p.eng] = v
                    E.wait_ge(sems[p.eng], v)
                else:
                    inst = a.fn(E)
                    if a.marked:
                        inst.then_inc(sems[a.eng], 16 if a.is_dma else 1)
            if ename == "sync" and sync_total:
                E.wait_ge(sems["sync"], sync_total)

        blk = nc.Block()
        block = blk.__enter__()
        block.tensor(lambda E: body(E, "pe"))
        block.scalar(lambda E: body(E, "act"))
        block.vector(lambda E: body(E, "dve"))
        block.gpsimd(lambda E: body(E, "pool"))
        block.sync(lambda E: body(E, "sync"))
        blk.__exit__(None, None, None)
        for cm in reversed(self._cm):
            cm.__exit__(None, None, None)


def col(tile, j):
    return tile[:, j:j + 1]


def host_layouts(inp):
    f = np.ascontiguousarray
    g = {}
    g["w_inT"] = f(inp["L0_attn_W_in"].T)
    g["convw_col"] = f(inp["L0_attn_conv_w"].reshape(DI, 4)
                       .reshape(16, 128, 4).transpose(1, 0, 2).reshape(128, 64))
    g["convb_col"] = f(inp["L0_attn_conv_b"].reshape(16, 128).T)
    g["alog_col"] = f(inp["L0_attn_A_log"]
                      .reshape(16, 128, NST).transpose(1, 0, 2).reshape(128, 256))
    g["d_col"] = f(inp["L0_attn_D_param"].reshape(16, 128).T)
    g["w_xprojT"] = f(inp["L0_attn_W_xproj"].T)
    g["w_dtT"] = f(inp["L0_attn_W_dt"].T)
    g["bdt_col"] = f(inp["L0_attn_b_dt"].reshape(16, 128).T)
    g["w_outT"] = f(inp["L0_attn_W_out"].T)
    g["lng_col"] = f(inp["L0_attn_ln_gamma"].reshape(8, 128).T)
    g["lnb_col"] = f(inp["L0_attn_ln_beta"].reshape(8, 128).T)
    g["w_fcT"] = f(inp["L0_ffn_w_fc"].T)
    g["w_projT"] = f(inp["L0_ffn_w_proj"].T)
    g["vT"] = f(inp["L0_ffn_v"].T)
    g["fbT"] = f(inp["L0_ffn_fast_b"].T)
    g["uT"] = f(inp["L0_ffn_u"].T)
    g["gate"] = f(np.broadcast_to(np.asarray(inp["L0_ffn_gate"], np.float32).reshape(1, 1), (128, 1)).copy())
    g["w_qkv"] = f(inp["L1_attn_W_qkv"])
    g["bqkv_col"] = f(inp["L1_attn_b_qkv"].reshape(24, 128).T)
    selz = np.zeros((16, 1024), np.float32)
    for hp in range(8):
        selz[2 * hp, hp * 128:hp * 128 + 64] = 1.0
        selz[2 * hp + 1, hp * 128 + 64:hp * 128 + 128] = 1.0
    g["selz"] = selz
    g["eye128"] = np.eye(128, dtype=np.float32)
    g["u1"] = f(inp["L1_ffn_U1"])
    g["v1"] = f(inp["L1_ffn_V1"])
    g["b1_col"] = f(inp["L1_ffn_b1"].reshape(32, 128).T)
    g["u2"] = f(inp["L1_ffn_U2"])
    g["v2"] = f(inp["L1_ffn_V2"])
    g["b2_col"] = f(inp["L1_ffn_b2"].reshape(8, 128).T)
    return g


WEIGHT_SHAPES = {
    "w_inT": (D, 4096), "convw_col": (128, 64), "convb_col": (128, 16),
    "alog_col": (128, 256), "d_col": (128, 16), "w_xprojT": (DI, 33),
    "w_dtT": (33, DI), "bdt_col": (128, 16), "w_outT": (DI, D),
    "lng_col": (128, 8), "lnb_col": (128, 8),
    "w_fcT": (D, 4096), "w_projT": (4096, D), "vT": (D, 16), "fbT": (16, 16),
    "uT": (16, 4096), "gate": (128, 1),
    "w_qkv": (D, 3072), "bqkv_col": (128, 24),
    "selz": (16, 1024), "eye128": (128, 128),
    "u1": (D, 256), "v1": (256, 4096), "b1_col": (128, 32),
    "u2": (4096, 256), "v2": (256, 1024), "b2_col": (128, 8),
}


class WTiles:

    def __init__(self, tiles, mw):
        self.tiles = tiles
        self.mw = mw

    def mm(self, k, m):
        j, off = divmod(m * 128, self.mw)
        t = self.tiles[k][j]
        return t[:, off:off + 128]

    def full(self, k, j=0):
        return self.tiles[k][j][:, :]


def load_bf16_tiles(ctx, name, hbm, R, C, stages, guard=()):
    ntile = (R + 127) // 128
    mw = min(C, 256)
    nj = (C + mw - 1) // mw
    tiles, ready = [], []
    prev_cast = [list(guard), list(guard)]
    lds, rows, lens = [], [], []

    def emit_cast(i, lag_ld):
        n = lens[i]
        stg = stages[i % 2]
        ctx.need("act", lds[i], lag_ld)
        cs = None
        for j in range(nj):
            cs = ctx.on("act", lambda E, t=rows[i][j], s=stg, n=n, j=j, mw=mw:
                        E.activation(out=t[:, :], in_=s[0:n, j * mw:(j + 1) * mw],
                                     func=AF.Copy))
        prev_cast[i % 2] = [cs]
        ready.append(cs)

    for i in range(ntile):
        r0, r1 = i * 128, min(R, (i + 1) * 128)
        rows.append([ctx.sb(f"{name}_b{i}_{j}", [r1 - r0, mw], BF16) for j in range(nj)])
        lens.append(r1 - r0)
        tiles.append(rows[i])
        if prev_cast[i % 2]:
            ctx.need("sync", *prev_cast[i % 2])
        lds.append(ctx.dma("sync", stages[i % 2][0:r1 - r0, 0:C], hbm[r0:r1, :]))
        if i >= 1:
            emit_cast(i - 1, lds[i])
    s2 = ntile % 2
    if prev_cast[s2]:
        ctx.need("sync", *prev_cast[s2])
    r0 = (ntile - 1) * 128
    dummy = ctx.dma("sync", stages[s2][0:lens[-1], 0:C], hbm[r0:r0 + lens[-1], :])
    emit_cast(ntile - 1, dummy)
    return WTiles(tiles, mw), ready


def emit_ln(ctx, sh, x_tiles, tc, ones_col, ones_row, ps_stat, ps_b, sq_stage,
            pfx, out_dtype=BF16, gamma=None, beta=None, combined=False,
            extra_need=(), psb_guard=(), bufs=None, xn_guard=()):
    eps = sh["eps_ap"]
    """Plain (or combined-double) LayerNorm over feature axis, feature-major.
    x_tiles: 8 f32 tiles [128, tc]. Returns (xn_tiles(out_dtype), last_actions).
    combined=True: LN then LN-with-gamma/beta fused analytically (var2 = var/(var+eps))."""
    nd = len(x_tiles)
    acts_sq = []
    mm = []
    for i, xt in enumerate(x_tiles):
        ctx.need("act", *extra_need)
        if i >= 2:
            ctx.need("act", mm[2 * (i - 2) + 1])
        a = ctx.on("act", lambda E, s=sq_stage[i % 2], x=xt: E.activation(
            out=s[:, 0:tc], in_=x[:, 0:tc], func=AF.Square))
        acts_sq.append(a)
        ctx.need("pe", *extra_need)
        mm.append(ctx.on("pe", lambda E, x=xt, i=i: E.matmul(
            ps_stat[0:1, 0:tc], ones_col[:, :], x[:, 0:tc],
            start=(i == 0), stop=(i == nd - 1))))
        ctx.need("pe", acts_sq[i])
        mm.append(ctx.on("pe", lambda E, s=sq_stage[i % 2], i=i: E.matmul(
            ps_stat[32:33, 0:tc], ones_col[:, :], s[:, 0:tc],
            start=(i == 0), stop=(i == nd - 1))))
    rows = bufs["rows"]
    mu, e2, var, srow = (rows[0:1, 0:tc], rows[0:1, tc:2 * tc],
                         rows[0:1, 2 * tc:3 * tc], rows[0:1, 3 * tc:4 * tc])
    v2r = rows[0:1, 4 * tc:5 * tc] if combined else None
    ctx.need("dve", mm[-2], mm[-1])
    ctx.on("dve", lambda E: E.tensor_scalar(out=mu, in0=ps_stat[0:1, 0:tc],
                                            scalar1=1.0 / D, scalar2=None, op0=ALU.mult))
    ctx.on("dve", lambda E: E.tensor_scalar(out=e2, in0=ps_stat[32:33, 0:tc],
                                            scalar1=1.0 / D, scalar2=None, op0=ALU.mult))
    ctx.on("dve", lambda E: E.tensor_tensor(out=var, in0=mu, in1=mu, op=ALU.mult))
    dv = ctx.on("dve", lambda E: E.tensor_tensor(out=var, in0=e2, in1=var, op=ALU.subtract))
    ctx.need("act", dv)
    a1 = ctx.on("act", lambda E: E.activation(out=srow, in_=var, func=AF.Sqrt, bias=eps))
    ctx.need("dve", a1)
    r1 = ctx.on("dve", lambda E: E.reciprocal(out=srow, in_=srow))
    if combined:
        ctx.on("dve", lambda E: E.tensor_tensor(out=v2r, in0=srow, in1=srow, op=ALU.mult))
        d2 = ctx.on("dve", lambda E: E.tensor_tensor(out=v2r, in0=var, in1=v2r, op=ALU.mult))
        ctx.need("act", d2)
        a2 = ctx.on("act", lambda E: E.activation(out=v2r, in_=v2r, func=AF.Sqrt, bias=eps))
        ctx.need("dve", a2)
        ctx.on("dve", lambda E: E.reciprocal(out=v2r, in_=v2r))
        r1 = ctx.on("dve", lambda E: E.tensor_tensor(out=srow, in0=srow, in1=v2r, op=ALU.mult))
    ctx.need("pe", r1, *psb_guard)
    b1 = ctx.on("pe", lambda E: E.matmul(ps_b[:, 0:tc], ones_row[:, :], mu,
                                         start=True, stop=True))
    b2 = ctx.on("pe", lambda E: E.matmul(ps_b[:, 512:512 + tc], ones_row[:, :], srow,
                                         start=True, stop=True))
    mub = bufs["mub"]
    ctx.need("act", b1, b2)
    ctx.on("act", lambda E: E.activation(out=mub[:, 0:tc], in_=ps_b[:, 0:tc],
                                         func=AF.Copy), label=pfx + "mubcp1")
    ev = ctx.on("act", lambda E: E.activation(out=mub[:, tc:2 * tc],
                                              in_=ps_b[:, 512:512 + tc], func=AF.Copy),
                label=pfx + "mubcp2")
    xn_tiles, outs = [], []
    for i, xt in enumerate(x_tiles):
        xn = bufs["xn"][i]
        tmp = sq_stage[i % 2]
        ctx.need("dve", ev, *xn_guard)
        ctx.on("dve", lambda E, x=xt, t=tmp: E.tensor_tensor(
            out=t[:, 0:tc], in0=x[:, 0:tc], in1=mub[:, 0:tc], op=ALU.subtract))
        if gamma is not None:
            ctx.on("dve", lambda E, t=tmp: E.tensor_tensor(
                out=t[:, 0:tc], in0=t[:, 0:tc], in1=mub[:, tc:2 * tc], op=ALU.mult))
            ctx.on("dve", lambda E, t=tmp, i=i: E.tensor_scalar(
                out=t[:, 0:tc], in0=t[:, 0:tc], scalar1=col(gamma, i), scalar2=col(beta, i),
                op0=ALU.mult, op1=ALU.add))
        else:
            ctx.on("dve", lambda E, t=tmp: E.tensor_tensor(
                out=t[:, 0:tc], in0=t[:, 0:tc], in1=mub[:, tc:2 * tc], op=ALU.mult),
                   label=f"{pfx}norm{i}")
        o = ctx.on("dve", lambda E, t=tmp, x2=xn: E.tensor_copy(
            out=x2[:, :], in_=t[:, 0:tc]), label=f"{pfx}cast{i}")
        xn_tiles.append(xn)
        outs.append(o)
    ctx.need("act", *outs)
    hop = ctx.on("act", lambda E: E.activation(out=rows[0:1, 2 * tc:2 * tc + 1],
                                               in_=rows[0:1, 0:1], func=AF.Copy))
    return xn_tiles, [hop] * len(outs)


P5BARS = set()


def emit_p5(ctx, g, x4T, outT, sh, xin_guard=None, dbg=None, dump_w=None):
    sb, ps, on, dma, need = ctx.sb, ctx.ps, ctx.on, ctx.dma, ctx.need
    bar = ctx.full_barrier
    stg = [sb("p5stg0", [128, 4096], F32), sb("p5stg1", [128, 4096], F32)]
    u1b, r1 = load_bf16_tiles(ctx, "u1", g["u1"], D, 256, stg)
    v1b, r2 = load_bf16_tiles(ctx, "v1", g["v1"], 256, 4096, stg, guard=r1[-2:])
    u2b, r3 = load_bf16_tiles(ctx, "u2", g["u2"], 4096, 256, stg, guard=r2[-2:])
    v2b, r4 = load_bf16_tiles(ctx, "v2", g["v2"], 256, 1024, stg, guard=r3[-2:])
    ctx.need("pe", r1[-1], r2[-1], r3[-1], r4[-1])
    if "w" in P5BARS:
        bar()
    b1c = sb("p5b1c", [128, 32], F32)
    ld_b1 = dma("sync", b1c[:, :], g["b1_col"][:, :])
    b2c = sb("p5b2c", [128, 8], F32)
    ld_b2 = dma("sync", b2c[:, :], g["b2_col"][:, :])

    ps_stat = ps("p5stat", [33, 512])
    ps_P = ps("p5P", [128, 1024])
    ps_b = None
    ps_H = ps("p5H", [128, 1024])
    ps_Q = ps("p5Q", [128, 1024])
    ps_b = ps_Q
    ps_O = ps_H

    x4sb = [[sb(f"p5x4_{s}_{i}", [128, TC], F32) for i in range(8)] for s in range(2)]
    sqst = [[sb(f"p5sq_{s}_{i}", [128, TC], F32) for i in range(2)] for s in range(2)]
    lnbufs = [dict(rows=sb(f"p5rows{s}", [1, 5 * TC], F32),
                   mub=sb(f"p5mub{s}", [128, 2 * TC], F32),
                   xn=[sb(f"p5xn_{s}_{i}", [128, TC], BF16) for i in range(8)])
              for s in range(2)]
    Pb = [sb(f"p5Pb{i}", [128, TC], BF16) for i in range(2)]
    Hb = [sb(f"p5Hb{i}", [128, TC], BF16) for i in range(2)]
    Qb = [sb(f"p5Qb{i}", [128, TC], BF16) for i in range(2)]
    ot = [sb(f"p5ot{i}", [128, TC], F32) for i in range(4)]
    slot_guard = [[], []]
    out_guard = [[] for _ in range(NCH)]
    ctx.need("pe", *sh["ones_ready"])
    prev_evP = []
    prev_evQ = []
    prev_last_oev = None
    xn_guard = [[], []]
    ot_store = [None] * 4

    for c in range(NCH):
        s = c % 2
        t0 = c * TC
        lds = []
        for i in range(8):
            if xin_guard is not None:
                need("sync", xin_guard[c])
            need("sync", *slot_guard[s])
            lds.append(dma("sync", x4sb[s][i][:, :], x4T[i * 128:(i + 1) * 128, t0:t0 + TC]))
        lds.append(dma("sync", sh["lag"][:, :], x4T[896:1024, t0:t0 + 128]))
        slot_guard[s] = []
        if "a" in P5BARS:
            bar()
        guards = slot_guard[s] + ([] if c < 2 else [])
        if NOLN:
            xnb = lnbufs[c]["xn"]
            xno = []
            for i in range(8):
                need("dve", lds[i], *xn_guard[s])
                xno.append(on("dve", lambda E, i=i, x2=xnb[i], x=x4sb[s][i]: E.tensor_copy(
                    out=x2[:, :], in_=x[:, :])))
        else:
            xnb, xno = emit_ln(ctx, sh, x4sb[s], TC, sh["ones_col"], sh["ones_row"],
                               ps_stat, ps_b, sqst[s], f"p5c{c}_", out_dtype=BF16,
                               extra_need=lds, psb_guard=prev_evQ, bufs=lnbufs[s],
                               xn_guard=xn_guard[s])
        xn_guard[s] = []
        dumps = []
        if dbg is not None and c in (0, 1):
            for i in range(8):
                need("sync", xno[i])
                dumps.append(dma("sync", dbg[f"xn{c}"][i * 128:(i + 1) * 128, :], xnb[i][:, :]))
            need("pe", *dumps)
        dbgP = dbg is not None and c in (0, 1)
        if "b" in P5BARS:
            bar()
        evP = []
        prev_evP = evP
        for m in range(2):
            for k in range(8):
                need("pe", xno[k])
                pe_done = on("pe", lambda E, m=m, k=k, xnb=xnb: E.matmul(
                    ps_P[:, m * 512:m * 512 + TC], u1b.mm(k, m),
                    xnb[k][:, :], start=(k == 0), stop=(k == 7)))
                if m == 1:
                    xn_guard[s].append(pe_done)
            need("act", pe_done, ld_b1)
            evP.append(on("act", lambda E, m=m: E.activation(
                out=Pb[m][:, :], in_=ps_P[:, m * 512:m * 512 + TC], func=AF.Copy)))
            if dbgP:
                need("sync", evP[-1])
                dma("sync", dbg[f"P{c}"][m * 128:(m + 1) * 128, :], Pb[m][:, :])
        if "c" in P5BARS:
            bar()
        evH_prev = [None, None]
        for m in range(32):
            hb = Hb[m % 2]
            for k in range(2):
                need("pe", evP[k])
                if m == 0 and prev_last_oev is not None:
                    need("pe", prev_last_oev)
                if evH_prev[m % 2] is not None:
                    need("pe", evH_prev[m % 2][1])
                hdone = on("pe", lambda E, m=m, k=k: E.matmul(
                    ps_H[:, (m % 2) * 512:(m % 2) * 512 + TC],
                    v1b.mm(k, m), Pb[k][:, :],
                    start=(k == 0), stop=(k == 1)))
            need("act", hdone)
            if evH_prev[m % 2] is not None:
                need("act", evH_prev[m % 2][2])
            ev = on("act", lambda E, hb=hb, m=m: E.activation(
                out=hb[:, :], in_=ps_H[:, (m % 2) * 512:(m % 2) * 512 + TC],
                func=AF.Gelu_apprx_tanh, bias=col(b1c, m)))
            qmm = None
            for m2 in range(2):
                need("pe", ev)
                qmm = on("pe", lambda E, m2=m2, m=m, hb=hb: E.matmul(
                    ps_Q[:, m2 * 512:m2 * 512 + TC], u2b.mm(m, m2),
                    hb[:, :], start=(m == 0), stop=(m == 31)))
            evH_prev[m % 2] = (None, ev, qmm)
        qdone = qmm
        evQ = []
        prev_evQ = evQ
        for m2 in range(2):
            need("act", qdone)
            evQ.append(on("act", lambda E, m2=m2: E.activation(
                out=Qb[m2][:, :], in_=ps_Q[:, m2 * 512:m2 * 512 + TC], func=AF.Copy)))
        if "d" in P5BARS:
            bar()
        prev_oev = None
        for m in range(8):
            for k in range(2):
                need("pe", evQ[k])
                if prev_oev is not None:
                    need("pe", prev_oev)
                odone = on("pe", lambda E, m=m, k=k: E.matmul(
                    ps_O[:, 0:TC], v2b.mm(k, m), Qb[k][:, :],
                    start=(k == 0), stop=(k == 1)))
            o = ot[m % 4]
            need("dve", odone, ld_b2)
            need("dve", ot_store[m % 4], ot_store[(m + 1) % 4])
            oev = on("dve", lambda E, o=o, m=m: E.tensor_scalar(
                out=o[:, :], in0=ps_O[:, 0:TC], scalar1=col(b2c, m), scalar2=None,
                op0=ALU.add))
            prev_oev = oev
            add = on("dve", lambda E, o=o, x4t=x4sb[s][m]: E.tensor_tensor(
                out=o[:, :], in0=o[:, :], in1=x4t[:, :], op=ALU.add))
            need("sync", add)
            st = dma("sync", outT[m * 128:(m + 1) * 128, t0:t0 + TC], o[:, :])
            ot_store[m % 4] = st
            out_guard[c].append(st)
            slot_guard[s].append(add)
        if "e" in P5BARS:
            bar()
        prev_last_oev = prev_oev
        slot_guard[s] += [qdone]
        if c == 0 and dbg is not None and "Plate" in dbg:
            first_xnb = xnb
    if dump_w is not None:
        need("sync", prev_last_oev, qdone)
        for nm, wt in (("u1", u1b), ("v1", v1b), ("u2", u2b), ("v2", v2b)):
            k = 0
            for row in wt.tiles:
                for t in row:
                    dma("sync", dump_w[nm][0:t.shape[0], k * 256:k * 256 + 256], t[:, :])
                    k += 1
        dma("sync", dump_w["b1"][:, :], b1c[:, :])
        dma("sync", dump_w["b2"][:, :], b2c[:, :])
    if dbg is not None and "Plate" in dbg:
        need("pe", prev_last_oev)
        for k in range(8):
            lmm = on("pe", lambda E, k=k: E.matmul(
                ps_P[:, 0:TC], u1b[k][:, 0:128], first_xnb[k][:, :],
                start=(k == 0), stop=(k == 7)))
        need("act", lmm)
        lev = on("act", lambda E: E.activation(out=Pb[0][:, :], in_=ps_P[:, 0:TC],
                                               func=AF.Copy))
        need("sync", lev)
        dma("sync", dbg["Plate"][:, :], Pb[0][:, :])
    return out_guard


def emit_shared(ctx):
    sh = {}
    sh["ones_col"] = ctx.sb("ones_col", [128, 1], F32)
    sh["ones_row"] = ctx.sb("ones_row", [1, 128], F32)
    cst = ctx.sb("consts", [128, 4], F32)
    a = ctx.on("dve", lambda E: E.memset(sh["ones_col"][:, :], 1.0))
    b = ctx.on("dve", lambda E: E.memset(sh["ones_row"][:, :], 1.0))
    c0 = ctx.on("dve", lambda E: E.memset(cst[:, 0:1], 1e-5))
    c1 = ctx.on("dve", lambda E: E.memset(cst[:, 1:2], 1.0))
    c2 = ctx.on("dve", lambda E: E.memset(cst[:, 2:3], 1e-6))
    c3 = ctx.on("dve", lambda E: E.memset(cst[:, 3:4], 0.001))
    sh["cst"] = cst
    sh["lag"] = ctx.sb("lag_tile", [128, 128], F32)
    sh["eps_ap"] = cst[0:1, 0:1]
    sh["ones_ready"] = [a, b, c0, c1, c2, c3]
    return sh


def build_p5_test(debug=False):
    nc = bass.Bass("TRN2", target_bir_lowering=False, debug=False)
    g = {}
    for k in ("u1", "v1", "b1_col", "u2", "v2", "b2_col"):
        g[k] = nc.dram_tensor(k, list(WEIGHT_SHAPES[k]), F32, kind="ExternalInput").ap()
    x4T = nc.dram_tensor("x4T", [D, T], F32, kind="ExternalInput").ap()
    outT = nc.dram_tensor("outT", [D, T], F32, kind="ExternalOutput").ap()
    dbg = None
    if debug:
        dbg = {f"xn{c}": nc.dram_tensor(f"dbg_xn{c}", [D, TC], BF16,
                                        kind="ExternalOutput").ap() for c in (0, 1)}
        dbg.update({f"P{c}": nc.dram_tensor(f"dbg_P{c}", [256, TC], BF16,
                                            kind="ExternalOutput").ap() for c in (0, 1)})
        dbg["Plate"] = nc.dram_tensor("dbg_Plate", [128, TC], BF16,
                                      kind="ExternalOutput").ap()
    ctx = Ctx(nc)
    sh = emit_shared(ctx)
    og = emit_p5(ctx, g, x4T, outT, sh, dbg=dbg)
    ctx.need("sync", *og[-1])
    ctx.build()
    return nc



MAGIC = 12582912.0


def emit_bitlinear_prep(ctx, g, key, R, C, sh, stg, pfx, guard=()):
    sb, on, dma, need = ctx.sb, ctx.on, ctx.dma, ctx.need
    bar = ctx.full_barrier
    nk = R // 128
    ncc = C // 512
    mw = min(C, 256)
    nj = C // mw
    tiles = [[sb(f"{pfx}we{i}_{j}", [128, mw], BF16) for j in range(nj)]
             for i in range(nk)]
    ctx.push()
    ps_m = ctx.ps(pfx + "psm", [128, 1024])
    rowbuf = sb(pfx + "rows", [1, 2 * C], F32)
    scl = rowbuf[0:1, 0:C]
    inv = rowbuf[0:1, C:2 * C]
    sclb = sb(pfx + "sclb", [128, C], F32)
    invb = sb(pfx + "invb", [128, C], F32)
    trscr = sb(pfx + "trscr", [128, C], F32)
    need("sync", *guard)
    for i in range(nk):
        s = stg[0]
        bar()
        dma("sync", s[:, 0:C], g[key][i * 128:(i + 1) * 128, :])
        bar()
        on("act", lambda E, s=s, C=C: E.activation(out=s[:, 0:C], in_=s[:, 0:C],
                                                   func=AF.Abs))
        bar()
        for cc in range(ncc):
            on("pe", lambda E, s=s, cc=cc: E.matmul(
                ps_m[0:1, (cc % 2) * 512:(cc % 2) * 512 + 512], sh["ones_col"][:, :],
                s[:, cc * 512:cc * 512 + 512], start=True, stop=True))
            bar()
            if i == 0:
                on("dve", lambda E, cc=cc: E.tensor_copy(
                    out=scl[0:1, cc * 512:cc * 512 + 512],
                    in_=ps_m[0:1, (cc % 2) * 512:(cc % 2) * 512 + 512]))
            else:
                on("dve", lambda E, cc=cc: E.tensor_tensor(
                    out=scl[0:1, cc * 512:cc * 512 + 512],
                    in0=scl[0:1, cc * 512:cc * 512 + 512],
                    in1=ps_m[0:1, (cc % 2) * 512:(cc % 2) * 512 + 512], op=ALU.add))
            bar()
    on("dve", lambda E: E.tensor_scalar(out=scl, in0=scl, scalar1=1.0 / R,
                                        scalar2=1e-5, op0=ALU.mult, op1=ALU.max))
    bar()
    on("dve", lambda E: E.reciprocal(out=inv, in_=scl))
    bar()
    for cc in range(ncc):
        for j2, (row, dstb) in enumerate(((scl, sclb), (inv, invb))):
            on("pe", lambda E, row=row, cc=cc, j2=j2: E.matmul(
                ps_m[0:128, j2 * 512:j2 * 512 + 512], sh["ones_row"][:, :],
                row[0:1, cc * 512:cc * 512 + 512], start=True, stop=True))
            bar()
            on("act", lambda E, dstb=dstb, cc=cc, j2=j2: E.activation(
                out=dstb[:, cc * 512:cc * 512 + 512],
                in_=ps_m[0:128, j2 * 512:j2 * 512 + 512], func=AF.Copy))
            bar()
    ready = []
    for i in range(nk):
        s = stg[0]
        bar()
        dma("sync", s[:, 0:C], g[key][i * 128:(i + 1) * 128, :])
        bar()
        on("dve", lambda E, s=s, C=C: E.tensor_tensor(
            out=trscr[:, 0:C], in0=s[:, 0:C], in1=invb[:, :], op=ALU.mult))
        bar()
        on("dve", lambda E, s=s, C=C: E.tensor_scalar(
            out=s[:, 0:C], in0=trscr[:, 0:C], scalar1=MAGIC, scalar2=None, op0=ALU.add))
        bar()
        on("dve", lambda E, s=s, C=C: E.tensor_scalar(
            out=trscr[:, 0:C], in0=s[:, 0:C], scalar1=-MAGIC, scalar2=None, op0=ALU.add))
        bar()
        on("dve", lambda E, s=s, C=C: E.tensor_scalar(
            out=s[:, 0:C], in0=trscr[:, 0:C], scalar1=1.0, scalar2=-1.0,
            op0=ALU.min, op1=ALU.max))
        bar()
        on("dve", lambda E, s=s, C=C: E.tensor_tensor(
            out=trscr[:, 0:C], in0=s[:, 0:C], in1=sclb[:, :], op=ALU.mult))
        bar()
        t = None
        for j in range(nj):
            t = on("act", lambda E, tl=tiles[i][j], j=j, mw=mw: E.activation(
                out=tl[:, :], in_=trscr[:, j * mw:(j + 1) * mw], func=AF.Copy))
        bar()
        ready.append(t)
    ctx.pop()
    return WTiles(tiles, mw), ready


def emit_p3(ctx, g, x1T, x2T, sh, xin_guard=None, dbg=None):
    sb, ps, on, dma, need = ctx.sb, ctx.ps, ctx.on, ctx.dma, ctx.need
    TCP = 256
    NCHP = T // TCP
    stg = [sb("p3stg0", [128, 4096], F32), sb("p3stg1", [128, 4096], F32)]
    w1b, w1r = emit_bitlinear_prep(ctx, g, "w_fcT", D, 4096, sh, stg, "p3w1")
    w2b, w2r = emit_bitlinear_prep(ctx, g, "w_projT", 4096, D, sh, stg, "p3w2",
                                   guard=w1r[-2:])
    vTb, r1 = load_bf16_tiles(ctx, "p3vT", g["vT"], D, 16, stg, guard=w2r[-2:])
    fbTb, r2 = load_bf16_tiles(ctx, "p3fbT", g["fbT"], 16, 16, stg, guard=r1[-2:])
    uTb, r3 = load_bf16_tiles(ctx, "p3uT", g["uT"], 16, 4096, stg, guard=r2[-1:])
    gcol = sb("p3gate", [128, 1], F32)
    ldg = dma("sync", gcol[:, :], g["gate"][:, :])
    if dbg is not None:
        need("sync", w1r[0])
        dma("sync", dbg["w1t0"][:, :], w1b.tiles[0][0][:, :])
        need("sync", w2r[0])
        dma("sync", dbg["w2t0"][:, :], w2b.tiles[0][0][:, :])
    ctx.need("pe", w1r[-1], w2r[-1], r1[-1], r2[-1], r3[-1], *sh["ones_ready"])

    ps_stat = ps("p3stat", [33, 512])
    ps_A = ps("p3A", [128, 1024])
    ps_b = ps("p3bc", [128, 1024])
    ps_B = ps("p3B", [128, 1024])
    ps_g = ps("p3g", [16, 512])

    _x1 = [sb(f"p3x1_{i}", [128, TCP], F32) for i in range(8)]
    x1sb = [_x1, _x1]
    sqst = [[stg[s][:, j * TCP:(j + 1) * TCP] for j in range(2)] for s in range(2)]
    _ln = dict(rows=sb("p3rows", [1, 4 * TCP], F32),
               mub=sb("p3mub", [128, 2 * TCP], F32),
               xn=[sb(f"p3xn_{i}", [128, TCP], BF16) for i in range(8)])
    lnbufs = [_ln, _ln]
    g1b = sb("p3g1b", [16, TCP], BF16)
    g2b = sb("p3g2b", [16, TCP], BF16)
    hsb = [stg[0][:, 2048:2048 + TCP], stg[0][:, 2048 + TCP:2048 + 2 * TCP]]
    r2b = [sb(f"p3r2b{m}", [128, TCP], BF16) for m in range(32)]
    ot = [sb(f"p3ot{i}", [128, TCP], F32) for i in range(2)]

    slot_guard = [[], []]
    xn_guard = [[], []]
    out_guard = [[] for _ in range(T // TCP)]
    prev_evb = []
    prev_g_ev = []
    prev_w1_start = []
    sq_acts = [None] * 32
    prev_sq_rd = [None] * 32
    prev_hev = [None, None]
    prev_g3 = [None, None]
    ot_store = [None] * 4
    prev_oev = [None, None]

    for c in range(NCHP):
        s = c % 2
        t0 = c * TCP
        lds = []
        for i in range(8):
            if xin_guard is not None:
                need("sync", xin_guard[c * TCP // TC])
            need("sync", *slot_guard[s])
            lds.append(dma("sync", x1sb[s][i][:, :],
                           x1T[i * 128:(i + 1) * 128, t0:t0 + TCP]))
        lds.append(dma("sync", sh["lag"][:, :], x1T[896:1024, t0:t0 + 128]))
        slot_guard[s] = []
        xnb, xno = emit_ln(ctx, sh, x1sb[s], TCP, sh["ones_col"], sh["ones_row"],
                           ps_stat, ps_b, sqst[s], f"p3c{c}_", out_dtype=BF16,
                           extra_need=lds, psb_guard=prev_evb, bufs=lnbufs[s],
                           xn_guard=xn_guard[s])
        xn_guard[s] = []
        if dbg is not None and c == 0:
            for i in range(8):
                need("sync", xno[i])
                dma("sync", dbg["xn0"][i * 128:(i + 1) * 128, :], xnb[i][:, :])
        gmm = None
        for k in range(8):
            need("pe", xno[k], *prev_g_ev)
            gmm = on("pe", lambda E, k=k, xnb=xnb: E.matmul(
                ps_g[0:16, 0:TCP], vTb.full(k), xnb[k][:, :],
                start=(k == 0), stop=(k == 7)))
        need("act", gmm)
        gev = on("act", lambda E: E.activation(out=g1b[:, :], in_=ps_g[0:16, 0:TCP],
                                               func=AF.Copy))
        need("pe", gev)
        gmm2 = on("pe", lambda E: E.matmul(ps_g[0:16, 0:TCP], fbTb.full(0),
                                           g1b[:, :], start=True, stop=True))
        need("act", gmm2)
        gev2 = on("act", lambda E: E.activation(out=g2b[:, :], in_=ps_g[0:16, 0:TCP],
                                                func=AF.Copy))
        prev_g_ev = [gev2]
        for m in range(32):
            b = m % 2
            w1mm = None
            for k in range(8):
                need("pe", xno[k], *prev_w1_start)
                if prev_hev[b] is not None:
                    need("pe", prev_hev[b])
                w1mm = on("pe", lambda E, m=m, k=k, b=b, xnb=xnb: E.matmul(
                    ps_A[:, b * 512:b * 512 + TCP],
                    w1b.mm(k, m), xnb[k][:, :],
                    start=(k == 0), stop=(k == 7)))
            need("pe", gev2)
            if prev_g3[b] is not None:
                need("pe", prev_g3[b])
            g3mm = on("pe", lambda E, m=m, b=b: E.matmul(
                ps_B[:, b * 512:b * 512 + TCP], uTb.mm(0, m),
                g2b[:, :], start=True, stop=True))
            need("act", w1mm, g3mm)
            if prev_sq_rd[m] is not None:
                need("act", prev_sq_rd[m])
            hev = on("act", lambda E, b=b, h=hsb[b]: E.activation(
                out=h[:, :], in_=ps_A[:, b * 512:b * 512 + TCP], func=AF.Copy))
            need("dve", hev, g3mm, ldg)
            st1 = on("dve", lambda E, b=b, h=hsb[b]: E.scalar_tensor_tensor(
                out=h[:, :], in0=ps_B[:, b * 512:b * 512 + TCP], scalar=gcol[:, 0:1],
                in1=h[:, :], op0=ALU.mult, op1=ALU.add))
            prev_g3[b] = st1
            rl = on("dve", lambda E, b=b, h=hsb[b]: E.tensor_scalar(
                out=h[:, :], in0=h[:, :], scalar1=0.0, scalar2=None, op0=ALU.max))
            need("act", rl)
            sq = on("act", lambda E, b=b, h=hsb[b], m=m: E.activation(
                out=r2b[m][:, :], in_=h[:, :], func=AF.Square))
            prev_hev[b] = sq
            sq_acts[m] = sq
            if dbg is not None and c == 0 and m == 0:
                need("sync", hev)
                dma("sync", dbg["h_w1"][:, :], hsb[0][:, :])
                need("sync", st1)
                dma("sync", dbg["h_full"][:, :], hsb[0][:, :])
                need("sync", sq)
                dma("sync", dbg["r2_0"][:, :], r2b[0][:, :])
                dma("sync", dbg["g2b"][:, :], g2b[:, :])
                dma("sync", dbg["g1b"][:, :], g1b[:, :])
        prev_w1_start = []
        for m2 in range(8):
            b = m2 % 2
            w2mm = None
            for k2 in range(32):
                need("pe", sq_acts[k2])
                if k2 == 0 and prev_oev[b] is not None:
                    need("pe", prev_oev[b])
                w2mm = on("pe", lambda E, m2=m2, k2=k2, b=b: E.matmul(
                    ps_A[:, b * 512:b * 512 + TCP],
                    w2b.mm(k2, m2), r2b[k2][:, :],
                    start=(k2 == 0), stop=(k2 == 31)))
            o = ot[m2 % 2]
            need("dve", w2mm, ot_store[m2 % 2])
            oev = on("dve", lambda E, o=o, b=b, x1t=x1sb[s][m2]: E.tensor_tensor(
                out=o[:, :], in0=ps_A[:, b * 512:b * 512 + TCP], in1=x1t[:, :],
                op=ALU.add))
            prev_oev[b] = oev
            need("sync", oev)
            st = dma("sync", x2T[m2 * 128:(m2 + 1) * 128, t0:t0 + TCP], o[:, :])
            ot_store[m2 % 2] = st
            out_guard[c].append(st)
            slot_guard[s].append(oev)
        prev_w1_start = [prev_oev[0], prev_oev[1]]
        prev_evb = xno
        prev_sq_rd = [w2mm] * 32
        prev_hev = [prev_oev[0], prev_oev[1]]
    return out_guard


def build_p3_test(debug=False):
    nc = bass.Bass("TRN2", target_bir_lowering=False, debug=False)
    g = {}
    for k in ("w_fcT", "w_projT", "vT", "fbT", "uT", "gate"):
        g[k] = nc.dram_tensor(k, list(WEIGHT_SHAPES[k]), F32, kind="ExternalInput").ap()
    x1T = nc.dram_tensor("x1T", [D, T], F32, kind="ExternalInput").ap()
    x2T = nc.dram_tensor("x2T", [D, T], F32, kind="ExternalOutput").ap()
    dbg = None
    if debug:
        dbg = dict(
            w1t0=nc.dram_tensor("dbg_w1t0", [128, 256], BF16, kind="ExternalOutput").ap(),
            w2t0=nc.dram_tensor("dbg_w2t0", [128, 256], BF16, kind="ExternalOutput").ap(),
            xn0=nc.dram_tensor("dbg_xn0", [D, 256], BF16, kind="ExternalOutput").ap(),
            r2_0=nc.dram_tensor("dbg_r2_0", [128, 256], BF16, kind="ExternalOutput").ap(),
            g2b=nc.dram_tensor("dbg_g2b", [16, 256], BF16, kind="ExternalOutput").ap(),
            g1b=nc.dram_tensor("dbg_g1b", [16, 256], BF16, kind="ExternalOutput").ap(),
            h_w1=nc.dram_tensor("dbg_h_w1", [128, 256], F32, kind="ExternalOutput").ap(),
            h_full=nc.dram_tensor("dbg_h_full", [128, 256], F32, kind="ExternalOutput").ap(),
        )
    ctx = Ctx(nc)
    sh = emit_shared(ctx)
    og = emit_p3(ctx, g, x1T, x2T, sh, dbg=dbg)
    ctx.need("sync", *og[-1])
    ctx.build()
    return nc


def emit_p4(ctx, g, x3T, x4T, sh, xin_guard=None, dbg=None):
    sb, ps, on, dma, need = ctx.sb, ctx.ps, ctx.on, ctx.dma, ctx.need
    bar = ctx.full_barrier
    NH = 16
    HD = 64
    stg = [sb("p4stg0", [128, 3072], F32), sb("p4stg1", [128, 3072], F32)]
    wqkv, wr = load_bf16_tiles(ctx, "p4qkv", g["w_qkv"], D, 3072, stg)
    bqc = sb("p4bq", [128, 24], F32)
    ldb = dma("sync", bqc[:, :], g["bqkv_col"][:, :])
    eyef = stg[0][:, 2560:2688]
    need("sync", wr[-1], wr[-2])
    lde = dma("sync", eyef[:, :], g["eye128"][:, :])
    eyeb = stg[0][:, 2688:2752].bitcast(BF16)
    need("act", lde)
    on("act", lambda E: E.activation(out=eyeb[:, :], in_=eyef[:, :], func=AF.Copy))
    selz = stg[1][0:16, 1536:2560]
    ones_bcol = sb("p4onesb", [128, 1], BF16)
    on("act", lambda E: E.activation(out=ones_bcol[:, :], in_=sh["ones_col"][:, :],
                                     func=AF.Copy))
    ctx.need("pe", wr[-1], ldb, *sh["ones_ready"])

    x3sb = [sb(f"p4x3_{i}", [128, TC], F32) for i in range(8)]
    sqst = [stg[0][:, 0:TC], stg[0][:, TC:2 * TC]]
    lnb = dict(rows=sb("p4rows", [1, 4 * TC], F32), mub=sb("p4mub", [128, 2 * TC], F32),
               xn=[sb(f"p4xn{i}", [128, TC], BF16) for i in range(8)])
    phit = [stg[0][:, 2 * TC:3 * TC], stg[0][:, 3 * TC:4 * TC]]
    phim = stg[0][:, 4 * TC:5 * TC]
    pq2 = [[sb(f"p4pq_{h}_{c}", [64, TC], BF16) for c in range(NCH)] for h in range(16)]
    pkb = [[sb(f"p4pk_{dt}_{st}", [128, 128], BF16) for st in range(4)] for dt in range(8)]
    vb = [[sb(f"p4v_{dt}_{st}", [128, 128], BF16) for st in range(4)] for dt in range(8)]
    pkT = [[sb(f"p4pkT_{st}_{j}", [128, 256], BF16) for j in range(4)] for st in range(4)]
    vT = [[sb(f"p4vT_{st}_{j}", [128, 256], BF16) for j in range(4)] for st in range(4)]
    kvsb = stg[1][0:64, 3 * TC:3 * TC + 1024]
    ksum_acc = [stg[1][0:1, 2560:3072], stg[1][64:65, 2560:3072]]
    kscol = stg[0][:, 2752:2760]
    need("dve", wr[-1], wr[-2])
    zk = on("dve", lambda E: E.memset(kvsb[:, :], 0.0))

    ctx.push()
    ps_stat = ps("p4stat", [33, 512])
    ps_b = ps("p4bc", [128, 1024])
    ps_qk = ps("p4qk", [128, 1024])
    ps_tr = ps("p4tr", [128, 256], BF16)
    ps_kv = ps("p4kv", [128, 1024])

    out_guard = [[] for _ in range(NCH)]

    for c in range(NCH):
        t0 = c * TC
        bar()
        lds = []
        for i in range(8):
            if xin_guard is not None:
                need("sync", xin_guard[c])
            lds.append(dma("sync", x3sb[i][:, :], x3T[i * 128:(i + 1) * 128, t0:t0 + TC]))
        lds.append(dma("sync", sh["lag"][:, :], x3T[896:1024, t0:t0 + 128]))
        xnb, xno = emit_ln(ctx, sh, x3sb, TC, sh["ones_col"], sh["ones_row"],
                           ps_stat, ps_b, sqst, f"p4c{c}_", out_dtype=BF16,
                           extra_need=lds, bufs=lnb)
        bar()
        for m in range(24):
            b = m % 2
            for k in range(8):
                mmq = on("pe", lambda E, m=m, k=k, b=b, xnb=xnb: E.matmul(
                    ps_qk[:, b * 512:b * 512 + TC], wqkv.mm(k, m), xnb[k][:, :],
                    start=(k == 0), stop=(k == 7)))
            need("dve", mmq)
            tv = phit[b]
            ev0 = on("dve", lambda E, b=b, m=m, tv=tv: E.tensor_scalar(
                out=tv[:, :], in0=ps_qk[:, b * 512:b * 512 + TC],
                scalar1=col(bqc, m), scalar2=None, op0=ALU.add))
            if m < 16:
                on("dve", lambda E, tv=tv: E.tensor_scalar(
                    out=phim[:, :], in0=tv[:, :], scalar1=0.0, scalar2=None,
                    op0=ALU.min))
                d1 = on("dve", lambda E: E.engine_nop())
                need("act", d1)
                a1 = on("act", lambda E: E.activation(out=phim[:, :], in_=phim[:, :],
                                                      func=AF.Exp))
                need("dve", a1)
                st1 = on("dve", lambda E, tv=tv: E.scalar_tensor_tensor(
                    out=tv[:, :], in0=tv[:, :], scalar=0.0, in1=phim[:, :],
                    op0=ALU.max, op1=ALU.add))
                dt = m % 8
                if m < 8:
                    need("act", st1)
                    for hh in range(2):
                        cq = on("act", lambda E, dt=dt, c=c, hh=hh, tv=tv: E.activation(
                            out=pq2[2 * dt + hh][c][:, :],
                            in_=tv[hh * 64:(hh + 1) * 64, :], func=AF.Copy))
                else:
                    need("act", st1)
                    for st in range(4):
                        ck = on("act", lambda E, dt=dt, st=st, tv=tv: E.activation(
                            out=pkb[dt][st][:, :], in_=tv[:, st * 128:(st + 1) * 128],
                            func=AF.Copy))

            else:
                dt = m - 16
                need("act", ev0)
                for st in range(4):
                    on("act", lambda E, dt=dt, st=st, tv=tv: E.activation(
                        out=vb[dt][st][:, :], in_=tv[:, st * 128:(st + 1) * 128],
                        func=AF.Copy))
            bar()
        for st in range(4):
            for dt in range(8):
                for src_t, dst in ((pkb, pkT), (vb, vT)):
                    tr = on("pe", lambda E, st=st, dt=dt, src_t=src_t: E.transpose(
                        ps_tr[:, 0:128], src_t[dt][st][:, :], eyeb[:, :]))
                    need("act", tr)
                    j, off = divmod(dt * 128, 256)
                    evt = on("act", lambda E, st=st, dst=dst, j=j, off=off: E.activation(
                        out=dst[st][j][:, off:off + 128], in_=ps_tr[:, 0:128],
                        func=AF.Copy))
                    need("pe", evt)
        bar()
        prev_kev = [None, None]
        for h in range(NH):
            b = h % 2
            if prev_kev[b] is not None:
                need("pe", prev_kev[b])
            jh, offh = divmod(h * 64, 256)
            for st in range(4):
                kmm = on("pe", lambda E, h=h, st=st, b=b, jh=jh, offh=offh: E.matmul(
                    ps_kv[0:64, b * 512:b * 512 + 64],
                    pkT[st][jh][:, offh:offh + 64], vT[st][jh][:, offh:offh + 64],
                    start=(st == 0), stop=(st == 3)))
            need("dve", kmm)
            kev = on("dve", lambda E, h=h, b=b: E.tensor_tensor(
                out=kvsb[:, h * 64:(h + 1) * 64], in0=kvsb[:, h * 64:(h + 1) * 64],
                in1=ps_kv[0:64, b * 512:b * 512 + 64], op=ALU.add))
            prev_kev[b] = kev
        bar()
        for j in range(4):
            for st in range(4):
                ksmm = on("pe", lambda E, j=j, st=st: E.matmul(
                    ps_kv[64:65, j * 256:(j + 1) * 256], ones_bcol[:, :],
                    pkT[st][j][:, :], start=(st == 0), stop=(st == 3)))
        bar()
        for r in range(2):
            if c == 0:
                on("act", lambda E, r=r: E.activation(
                    out=ksum_acc[r], in_=ps_kv[64:65, r * 512:(r + 1) * 512],
                    func=AF.Copy))
            else:
                on("dve", lambda E, r=r: E.tensor_tensor(
                    out=ksum_acc[r], in0=ksum_acc[r],
                    in1=ps_kv[64:65, r * 512:(r + 1) * 512], op=ALU.add))
        bar()

    bar()
    if dbg is not None:
        dma("sync", dbg["kvsb"][:, :], kvsb[:, :])
        bar()
    ctx.pop()
    kvTb = [stg[0][0:64, j * 128:(j + 1) * 128].bitcast(BF16) for j in range(4)]
    for j in range(4):
        on("act", lambda E, j=j: E.activation(out=kvTb[j][0:64, :],
                                              in_=kvsb[:, j * 256:(j + 1) * 256],
                                              func=AF.Copy))
    bar()
    lsz = dma("sync", selz[:, :], g["selz"][:, :])
    ksE = [stg[0][0:64, 512 + h * 8:512 + h * 8 + 8].bitcast(BF16) for h in range(16)]
    kss_d = ctx.nc.dram_tensor("p4kss", [2, 512], F32, kind="Internal").ap()
    dma("sync", kss_d[0:1, :], ksum_acc[0])
    dma("sync", kss_d[1:2, :], ksum_acc[1])
    bar()
    dma("sync", kscol[:, :], kss_d.rearrange("r (a p) -> p (r a)", p=128), allow_slow_non_contiguous=True)
    bar()
    for h in range(16):
        on("dve", lambda E, h=h: E.memset(ksE[h][:, :], 0.0))
    bar()
    for h in range(16):
        dt, hh = divmod(h, 2)
        on("act", lambda E, h=h, dt=dt, hh=hh: E.activation(
            out=ksE[h][0:64, h:h + 1],
            in_=kscol[hh * 64:(hh + 1) * 64, dt:dt + 1], func=AF.Copy))
    bar()
    zsb = sb("p4zsb", [16, TC], F32)
    zb = stg[1][:, 0:TC]
    att = [stg[1][:, TC:2 * TC], stg[1][:, 2 * TC:3 * TC]]
    ps_z = ps("p4z", [16, 512])
    ps_at = ps("p4at", [128, 1024])
    ps_zb = ps("p4zbb", [128, 512])
    for c in range(NCH):
        t0 = c * TC
        for h in range(16):
            zmm = on("pe", lambda E, c=c, h=h: E.matmul(
                ps_z[0:16, 0:TC], ksE[h][:, :], pq2[h][c][:, :],
                start=(h == 0), stop=(h == 15)))
        need("dve", zmm)
        on("dve", lambda E: E.tensor_scalar(
            out=zsb[:, 0:TC], in0=ps_z[0:16, 0:TC],
            scalar1=col(sh["cst"], 2)[0:16, :], scalar2=None, op0=ALU.add))
        zr = on("dve", lambda E: E.reciprocal(
            out=zsb[:, 0:TC], in_=zsb[:, 0:TC]))
        bar()
        lds = []
        for i in range(8):
            lds.append(dma("sync", x3sb[i][:, :], x3T[i * 128:(i + 1) * 128, t0:t0 + TC]))
        lds.append(dma("sync", sh["lag"][:, :], x3T[896:1024, t0:t0 + 128]))
        bar()
        for hp in range(8):
            b = hp % 2
            for hh in range(2):
                h = hp * 2 + hh
                j, off = divmod(h * 64, 256)
                amm = on("pe", lambda E, h=h, hh=hh, b=b, c=c, j=j, off=off: E.matmul(
                    ps_at[hh * 64:hh * 64 + 64, b * 512:b * 512 + TC],
                    kvTb[j][:, off:off + 64], pq2[h][c][:, :],
                    start=True, stop=True))
            need("pe", zr, lsz)
            zbm = on("pe", lambda E, hp=hp: E.matmul(
                ps_zb[:, 0:TC], selz[:, hp * 128:(hp + 1) * 128], zsb[:, 0:TC],
                start=True, stop=True))
            need("act", zbm)
            zbe = on("act", lambda E: E.activation(out=zb[:, :], in_=ps_zb[:, 0:TC],
                                                   func=AF.Copy))
            need("dve", amm, zbe, *lds)
            a1 = on("dve", lambda E, b=b, hp=hp: E.tensor_tensor(
                out=att[b][:, :], in0=ps_at[:, b * 512:b * 512 + TC], in1=zb[:, :],
                op=ALU.mult))
            a2 = on("dve", lambda E, b=b, hp=hp: E.tensor_tensor(
                out=att[b][:, :], in0=att[b][:, :], in1=x3sb[hp][:, :], op=ALU.add))
            need("sync", a2)
            stx = dma("sync", x4T[hp * 128:(hp + 1) * 128, t0:t0 + TC], att[b][:, :])
            out_guard[c].append(stx)
            bar()
    return out_guard


def build_p4_test():
    nc = bass.Bass("TRN2", target_bir_lowering=False, debug=False)
    g = {}
    for k in ("w_qkv", "bqkv_col", "selz", "eye128"):
        g[k] = nc.dram_tensor(k, list(WEIGHT_SHAPES[k]), F32, kind="ExternalInput").ap()
    x3T = nc.dram_tensor("x3T", [D, T], F32, kind="ExternalInput").ap()
    x4T = nc.dram_tensor("x4T", [D, T], F32, kind="ExternalOutput").ap()
    dbg = dict(kvsb=nc.dram_tensor("dbg_kvsb", [64, 1024], F32, kind="ExternalOutput").ap(),
               ksum=nc.dram_tensor("dbg_ksum", [128, 8], F32, kind="ExternalOutput").ap())
    ctx = Ctx(nc)
    sh = emit_shared(ctx)
    og = emit_p4(ctx, g, x3T, x4T, sh, dbg=dbg)
    ctx.need("sync", *og[-1])
    ctx.build()
    return nc


def emit_p1(ctx, g, xT, scr, sh):
    sb, ps, on, dma, need = ctx.sb, ctx.ps, ctx.on, ctx.dma, ctx.need
    bar = ctx.full_barrier
    stg = [sb("p1stg0", [128, 4096], F32), sb("p1stg1", [128, 4096], F32)]
    win, wr1 = load_bf16_tiles(ctx, "p1win", g["w_inT"], D, 4096, stg)
    wxp, wr2 = load_bf16_tiles(ctx, "p1wxp", g["w_xprojT"], DI, 33, stg, guard=wr1[-2:])
    wdt, wr3 = load_bf16_tiles(ctx, "p1wdt", g["w_dtT"], 33, DI, stg, guard=wr2[-2:])
    cwc = sb("p1cw", [128, 64], F32)
    cbc = sb("p1cb", [128, 16], F32)
    bdtc = sb("p1bdt", [128, 16], F32)
    lngc = sb("p1lng", [128, 8], F32)
    lnbc = sb("p1lnb", [128, 8], F32)
    lcw = dma("sync", cwc[:, :], g["convw_col"][:, :])
    dma("sync", cbc[:, :], g["convb_col"][:, :])
    dma("sync", bdtc[:, :], g["bdt_col"][:, :])
    dma("sync", lngc[:, :], g["lng_col"][:, :])
    lcl = dma("sync", lnbc[:, :], g["lnb_col"][:, :])
    ctx.need("pe", wr3[-1], lcl, *sh["ones_ready"])
    ctx.need("dve", wr3[-1], lcl)
    ctx.need("act", wr3[-1], lcl)

    ps_stat = ps("p1stat", [33, 512])
    ps_b = ps("p1bc", [128, 1024])
    ps_A = ps("p1A", [128, 1024])
    ps_xp = ps("p1xp", [33, 512])
    ps_dt = ps("p1dt", [128, 1024])

    xsb = [sb(f"p1x_{i}", [128, TC], F32) for i in range(8)]
    sqst = [stg[0][:, 0:TC], stg[0][:, TC:2 * TC]]
    lnb = dict(rows=sb("p1rows", [1, 5 * TC], F32), mub=sb("p1mub", [128, 2 * TC], F32),
               xn=[sb(f"p1xn{i}", [128, TC], BF16) for i in range(8)])
    xsm = [sb(f"p1xsm{cb}", [128, TC + 3], F32) for cb in range(16)]
    xsp = [sb(f"p1xsp{cb}", [128, TC], F32) for cb in range(16)]
    bounce = [stg[0][:, 2 * TC:3 * TC], stg[0][:, 3 * TC:4 * TC]]
    convt = [stg[0][:, 4 * TC:5 * TC], stg[0][:, 5 * TC:6 * TC]]
    xpf = stg[1][0:33, 1536:1536 + TC]
    xpb = stg[1][0:33, 2048:2048 + TC // 2].bitcast(BF16)
    xspb = [stg[1][:, 0:TC].bitcast(BF16)[:, 0:TC],
            stg[1][:, TC // 2:TC // 2 + TC].bitcast(BF16)[:, 0:TC]]
    etile = [stg[1][:, TC:2 * TC], stg[1][:, 2 * TC:3 * TC]]

    for cb in range(16):
        on("dve", lambda E, cb=cb: E.memset(xsm[cb][:, 0:3], 0.0))
    bar()
    prev_wev = [None, None]
    prev_cs = [None, None]
    prev_xmm = [None, None]
    prev_xev = None
    prev_e2 = [None, None]
    bq_st = [None, None]
    xsp_st = [None] * 16
    tail_cp = [None] * 16
    bc_st = [None, None]
    l_st = [None, None]

    for c in range(NCH):
        t0 = c * TC
        lds = []
        for i in range(8):
            lds.append(dma("sync", xsb[i][:, :], xT[i * 128:(i + 1) * 128, t0:t0 + TC]))
        lds.append(dma("sync", sh["lag"][:, :], xT[896:1024, t0:t0 + 128]))
        xnb, xno = emit_ln(ctx, sh, xsb, TC, sh["ones_col"], sh["ones_row"],
                           ps_stat, ps_b, sqst, f"p1c{c}_", out_dtype=BF16,
                           extra_need=lds, bufs=lnb, gamma=lngc, beta=lnbc,
                           combined=True)
        for m in range(32):
            b = m % 2
            for k in range(8):
                need("pe", xno[k])
                if k == 0 and prev_wev[b] is not None:
                    need("pe", prev_wev[b])
                wmm = on("pe", lambda E, m=m, k=k, b=b, xnb=xnb: E.matmul(
                    ps_A[:, b * 512:b * 512 + TC], win.mm(k, m), xnb[k][:, :],
                    start=(k == 0), stop=(k == 7)))
            need("act", wmm)
            if m < 16:
                ev = on("act", lambda E, m=m, b=b: E.activation(
                    out=xsm[m][:, 3:3 + TC], in_=ps_A[:, b * 512:b * 512 + TC],
                    func=AF.Copy))
            else:
                bo = bounce[b]
                need("act", bq_st[b])
                ev = on("act", lambda E, m=m, b=b, bo=bo: E.activation(
                    out=bo[:, :], in_=ps_A[:, b * 512:b * 512 + TC], func=AF.Silu))
                need("sync", ev)
                bq_st[b] = dma("sync",
                               scr["sz"][(m - 16) * 128:(m - 15) * 128, t0:t0 + TC],
                               bo[:, :])
            prev_wev[b] = ev
        for cb in range(16):
            b = cb % 2
            tmp = convt[b]
            need("dve", prev_wev[0], prev_wev[1], prev_cs[b])
            on("dve", lambda E, cb=cb, tmp=tmp: E.tensor_scalar(
                out=tmp[:, :], in0=xsm[cb][:, 0:TC], scalar1=col(cwc, cb * 4),
                scalar2=None, op0=ALU.mult))
            dvl = None
            for k in range(1, 4):
                dvl = on("dve", lambda E, cb=cb, k=k, tmp=tmp: E.scalar_tensor_tensor(
                    out=tmp[:, :], in0=xsm[cb][:, k:k + TC], scalar=col(cwc, cb * 4 + k),
                    in1=tmp[:, :], op0=ALU.mult, op1=ALU.add))
            need("act", dvl, xsp_st[cb])
            cs = on("act", lambda E, cb=cb, tmp=tmp: E.activation(
                out=xsp[cb][:, :], in_=tmp[:, :], func=AF.Silu, bias=col(cbc, cb)))
            prev_cs[b] = cs
            need("sync", cs)
            xsp_st[cb] = dma("sync", scr["xsp"][cb * 128:(cb + 1) * 128, t0:t0 + TC],
                             xsp[cb][:, :])
            tl = on("dve", lambda E, cb=cb: E.tensor_copy(
                out=xsm[cb][:, 0:3], in_=xsm[cb][:, TC:TC + 3]))
            tail_cp[cb] = tl
        prev_wev = [tail_cp[15], tail_cp[15]]
        for cb in range(16):
            b = cb % 2
            need("act", prev_cs[0], prev_cs[1])
            if prev_xmm[b] is not None:
                need("act", prev_xmm[b])
            cst = on("act", lambda E, cb=cb, b=b: E.activation(
                out=xspb[b][:, :], in_=xsp[cb][:, :], func=AF.Copy))
            need("pe", cst)
            if cb == 0:
                need("pe", prev_xev)
            xmm = on("pe", lambda E, cb=cb, b=b: E.matmul(
                ps_xp[0:33, 0:TC], wxp.full(cb), xspb[b][:, :],
                start=(cb == 0), stop=(cb == 15)))
            prev_xmm[b] = xmm
        need("act", xmm, bc_st[0], bc_st[1])
        xev = on("act", lambda E: E.activation(out=xpf[:, :], in_=ps_xp[0:33, 0:TC],
                                               func=AF.Copy))
        prev_xev = xev
        need("sync", xev)
        bc_st[0] = dma("sync", scr["B"][:, t0:t0 + TC], xpf[1:17, :])
        bc_st[1] = dma("sync", scr["C"][:, t0:t0 + TC], xpf[17:33, :])
        xcb = on("act", lambda E: E.activation(out=xpb[:, :], in_=xpf[:, :],
                                               func=AF.Copy))
        for m in range(16):
            b = m % 2
            need("pe", xcb)
            if prev_e2[b] is not None:
                need("pe", prev_e2[b])
            dmm = on("pe", lambda E, m=m, b=b: E.matmul(
                ps_dt[:, b * 512:b * 512 + TC], wdt.mm(0, m), xpb[:, :],
                start=True, stop=True))
            need("act", dmm, l_st[b])
            e1 = on("act", lambda E, m=m, b=b: E.activation(
                out=etile[b][:, :], in_=ps_dt[:, b * 512:b * 512 + TC],
                func=AF.Exp, bias=col(bdtc, m)))
            e2 = on("act", lambda E, b=b: E.activation(
                out=etile[b][:, :], in_=etile[b][:, :], func=AF.Ln,
                bias=col(sh["cst"], 1)))
            prev_e2[b] = e2
            need("sync", e2)
            l_st[b] = dma("sync", scr["l"][m * 128:(m + 1) * 128, t0:t0 + TC],
                          etile[b][:, :])
    return None


TS2 = 32
P2BARS = set()


def emit_p2(ctx, g, xT, scr, x1T, sh):
    sb, ps, on, dma, need = ctx.sb, ctx.ps, ctx.on, ctx.dma, ctx.need
    bar = ctx.full_barrier
    stg = [sb("p2stg0", [128, 1024], F32), sb("p2stg1", [128, 1024], F32)]
    wout, wro = load_bf16_tiles(ctx, "p2wout", g["w_outT"], DI, D, stg)
    alog = sb("p2alog", [128, 256], F32)
    A2 = sb("p2A2", [128, 256], F32)
    R2n = sb("p2R2n", [128, 256], F32)
    dcol = sb("p2dcol", [128, 16], F32)
    la = dma("sync", alog[:, :], g["alog_col"][:, :])
    ld = dma("sync", dcol[:, :], g["d_col"][:, :])
    need("act", la)
    ae = on("act", lambda E: E.activation(out=A2[:, :], in_=alog[:, :], func=AF.Exp))
    need("dve", ae, ld)
    on("dve", lambda E: E.tensor_scalar(out=A2[:, :], in0=A2[:, :], scalar1=-1.0,
                                        scalar2=None, op0=ALU.mult))
    on("dve", lambda E: E.tensor_scalar(out=R2n[:, :], in0=A2[:, :], scalar1=1e-8,
                                        scalar2=None, op0=ALU.add))
    on("dve", lambda E: E.reciprocal(out=R2n[:, :], in_=R2n[:, :]))
    on("dve", lambda E: E.tensor_scalar(out=R2n[:, :], in0=R2n[:, :], scalar1=-1.0,
                                        scalar2=None, op0=ALU.mult))
    ctx.need("pe", wro[-1], *sh["ones_ready"])
    bar()

    NSC = T // TS2
    dA = sb("p2dA", [128, 16 * 16 * TS2], F32)
    dBx = sb("p2dBx", [128, 16 * 16 * TS2], F32)
    hh = sb("p2hh", [128, 256 * TS2], F32)
    hprev = sb("p2hprev", [128, 256], F32)
    lch = sb("p2lch", [128, 16 * TS2], F32)
    xch = sb("p2xch", [128, 16 * TS2], F32)
    szch = sb("p2szch", [128, 16 * TS2], F32)
    Brow = sb("p2Brow", [1, 16 * TS2], F32)
    Crow = sb("p2Crow", [1, 16 * TS2], F32)
    Bb = sb("p2Bb", [128, 16 * TS2], F32)
    Cb = sb("p2Cb", [128, 16 * TS2], F32)
    ytile = sb("p2y", [128, 16 * TS2], F32)
    yzq = [sb(f"p2yzq{cb}", [128, 128], BF16) for cb in range(16)]
    xin = [sb(f"p2xin{i}", [128, 128], F32) for i in range(8)]
    ot = [sb(f"p2ot{i}", [128, 128], F32) for i in range(2)]
    ps_bc = ps("p2bc", [128, 1024])
    ps_o = ps("p2o", [128, 1024])

    ydx = sb("p2ydx", [128, 16 * TS2], F32)
    on("dve", lambda E: E.memset(hprev[:, :], 0.0))
    bar()
    chunk_war = []
    prev_bc_ev = []
    prev_cst = None
    prev_omm = None
    prev_out_ev = [None, None]
    prev_oev_all = []
    ot_st = [None, None]
    yzq_casts = []

    for sc in range(NSC):
        t0 = sc * TS2
        need("sync", *chunk_war)
        l1 = dma("sync", lch[:, :].rearrange("p (cb t) -> p cb t", cb=16),
                 scr["l"][:, t0:t0 + TS2].rearrange("(cb p) t -> p cb t", p=128))
        l2 = dma("sync", xch[:, :].rearrange("p (cb t) -> p cb t", cb=16),
                 scr["xsp"][:, t0:t0 + TS2].rearrange("(cb p) t -> p cb t", p=128))
        l3 = dma("sync", szch[:, :].rearrange("p (cb t) -> p cb t", cb=16),
                 scr["sz"][:, t0:t0 + TS2].rearrange("(cb p) t -> p cb t", p=128))
        l4 = dma("sync", Brow[:, :].rearrange("o (n t) -> o n t", n=16),
                 scr["B"][:, t0:t0 + TS2].unsqueeze(0))
        l5 = dma("sync", Crow[:, :].rearrange("o (n t) -> o n t", n=16),
                 scr["C"][:, t0:t0 + TS2].unsqueeze(0))
        lagd = dma("sync", sh["lag"][0:16, 0:TS2], scr["C"][:, t0:t0 + TS2])
        if "a" in P2BARS:
            bar()
        bcev = []
        for j, (row, ldx, dst) in enumerate(((Brow, l4, Bb), (Crow, l5, Cb))):
            need("pe", ldx, lagd, *prev_bc_ev)
            bm = on("pe", lambda E, row=row, j=j: E.matmul(
                ps_bc[:, j * 512:j * 512 + 16 * TS2], sh["ones_row"][:, :], row[0:1, :],
                start=True, stop=True))
            need("act", bm)
            bcev.append(on("act", lambda E, dst=dst, j=j: E.activation(
                out=dst[:, :], in_=ps_bc[:, j * 512:j * 512 + 16 * TS2], func=AF.Copy)))
        prev_bc_ev = bcev
        if "b" in P2BARS:
            bar()
        d4 = lambda slab: slab[:, :].rearrange("p (cb n t) -> p cb n t", cb=16, n=16)
        lb = lambda buf: buf[:, :].rearrange("p (cb t) -> p cb t", cb=16) \
            .unsqueeze(2).broadcast_to([128, 16, 16, TS2])
        cb4 = lambda buf: buf[:, :].rearrange("p (cb n) -> p cb n", cb=16) \
            .unsqueeze(3).broadcast_to([128, 16, 16, TS2])
        need("dve", l1, l2, lagd)
        for cb in range(16):
            stt1 = on("dve", lambda E, cb=cb: E.scalar_tensor_tensor(
                out=dA[:, cb * 16 * TS2:(cb + 1) * 16 * TS2].rearrange(
                    "p (n t) -> p n t", n=16),
                in0=lch[:, cb * TS2:(cb + 1) * TS2].unsqueeze(1)
                    .broadcast_to([128, 16, TS2]),
                scalar=0.001, in1=A2[:, cb * 16:(cb + 1) * 16].unsqueeze(2)
                    .broadcast_to([128, 16, TS2]),
                op0=ALU.add, op1=ALU.mult))
        need("act", stt1)
        ex = on("act", lambda E: E.activation(out=dA[:, :], in_=dA[:, :], func=AF.Exp))
        need("dve", ex)
        for cb in range(16):
            on("dve", lambda E, cb=cb: E.scalar_tensor_tensor(
                out=dBx[:, cb * 16 * TS2:(cb + 1) * 16 * TS2].rearrange(
                    "p (n t) -> p n t", n=16),
                in0=dA[:, cb * 16 * TS2:(cb + 1) * 16 * TS2].rearrange(
                    "p (n t) -> p n t", n=16), scalar=-1.0,
                in1=R2n[:, cb * 16:(cb + 1) * 16].unsqueeze(2)
                    .broadcast_to([128, 16, TS2]),
                op0=ALU.add, op1=ALU.mult))
        on("dve", lambda E: E.tensor_tensor(
            out=d4(dBx), in0=d4(dBx), in1=lb(xch), op=ALU.mult))
        need("dve", bcev[0])
        on("dve", lambda E: E.tensor_tensor(
            out=d4(dBx), in0=d4(dBx),
            in1=Bb[:, :].rearrange("p (n t) -> p n t", n=16).unsqueeze(1)
                .broadcast_to([128, 16, 16, TS2]), op=ALU.mult))
        if "c" in P2BARS:
            bar()
        def _st(slab, t):
            return slab[:, :].rearrange("p (cb n t2) -> p cb n t2",
                                        cb=16, n=16, t2=TS2)[:, :, :, t:t + 1]

        def _hv(t):
            return hh[:, t * 256:t * 256 + 256].rearrange(
                "p (cb n) -> p cb n", cb=16).unsqueeze(3)
        for t in range(TS2):
            hsrc = (hprev[:, :].rearrange("p (cb n) -> p cb n", cb=16).unsqueeze(3)
                    if t == 0 else _hv(t - 1))
            on("dve", lambda E, t=t, hsrc=hsrc: E.tensor_tensor(
                out=_hv(t), in0=hsrc, in1=_st(dA, t), op=ALU.mult))
            on("dve", lambda E, t=t: E.tensor_tensor(
                out=_hv(t), in0=_hv(t), in1=_st(dBx, t), op=ALU.add))
            on("dve", lambda E, t=t: E.tensor_scalar(
                out=hh[:, t * 256:t * 256 + 256],
                in0=hh[:, t * 256:t * 256 + 256],
                scalar1=10.0, scalar2=-10.0, op0=ALU.min, op1=ALU.max))
        on("dve", lambda E: E.tensor_copy(
            out=hprev[:, :], in_=hh[:, (TS2 - 1) * 256:TS2 * 256]))
        if "d" in P2BARS:
            bar()
        CbT = Cb[:, :].rearrange("p (n t) -> p t n", n=16)
        ytT = ytile[:, :].rearrange("p (cb t) -> p t cb", cb=16)
        need("dve", bcev[1], prev_cst)
        for t8 in range(TS2 // 8):
            on("dve", lambda E, t8=t8: E.tensor_tensor(
                out=dBx[:, t8 * 2048:(t8 + 1) * 2048].rearrange(
                    "p (t cb n) -> p t cb n", t=8, cb=16),
                in0=hh[:, t8 * 8 * 256:(t8 + 1) * 8 * 256].rearrange(
                    "p (t cb n) -> p t cb n", t=8, cb=16),
                in1=CbT[:, t8 * 8:(t8 + 1) * 8, :].unsqueeze(2)
                    .broadcast_to([128, 8, 16, 16]), op=ALU.mult))
            on("dve", lambda E, t8=t8: E.tensor_reduce(
                out=ytT[:, t8 * 8:(t8 + 1) * 8, :].unsqueeze(3),
                in_=dBx[:, t8 * 2048:(t8 + 1) * 2048].rearrange(
                    "p (t cb n) -> p t cb n", t=8, cb=16),
                axis=AX.X, op=ALU.add))
        on("dve", lambda E: E.tensor_tensor(
            out=ydx[:, :].rearrange("p (cb t) -> p cb t", cb=16),
            in0=xch[:, :].rearrange("p (cb t) -> p cb t", cb=16),
            in1=dcol[:, :].unsqueeze(2).broadcast_to([128, 16, TS2]), op=ALU.mult))
        on("dve", lambda E: E.tensor_tensor(
            out=ytile[:, :], in0=ytile[:, :], in1=ydx[:, :], op=ALU.add))
        need("dve", l3)
        yz = on("dve", lambda E: E.tensor_tensor(
            out=ytile[:, :], in0=ytile[:, :], in1=szch[:, :], op=ALU.mult))
        chunk_war = [yz] + bcev
        need("act", yz)
        q = sc % 4
        cst_last = None
        for cb in range(16):
            if q == 0 and prev_omm is not None:
                need("act", prev_omm)
            cst_last = on("act", lambda E, cb=cb, q=q: E.activation(
                out=yzq[cb][:, q * TS2:(q + 1) * TS2],
                in_=ytile[:, cb * TS2:(cb + 1) * TS2], func=AF.Copy))
        yzq_casts.append(cst_last)
        prev_cst = cst_last
        if "e" in P2BARS:
            bar()
        if q == 3:
            tg0 = (sc - 3) * TS2
            lxs = []
            for i in range(8):
                need("sync", prev_out_ev[0], prev_out_ev[1])
                lxs.append(dma("sync", xin[i][:, :],
                               xT[i * 128:(i + 1) * 128, tg0:tg0 + 128]))
            for m in range(8):
                b = m % 2
                for k in range(16):
                    need("pe", yzq_casts[-1])
                    if k == 0 and prev_out_ev[b] is not None:
                        need("pe", prev_out_ev[b])
                    omm = on("pe", lambda E, m=m, k=k, b=b: E.matmul(
                        ps_o[:, b * 512:b * 512 + 128], wout.mm(k, m), yzq[k][:, :],
                        start=(k == 0), stop=(k == 15)))
                need("dve", omm, lxs[m], ot_st[b], ot_st[1 - b])
                oev = on("dve", lambda E, m=m, b=b: E.tensor_tensor(
                    out=ot[b][:, :], in0=ps_o[:, b * 512:b * 512 + 128],
                    in1=xin[m][:, :], op=ALU.add))
                prev_out_ev[b] = oev
                need("sync", oev)
                st = dma("sync", x1T[m * 128:(m + 1) * 128, tg0:tg0 + 128], ot[b][:, :])
                ot_st[b] = st
                prev_oev_all.append(st)
            prev_omm = omm
            yzq_casts = []
    return None


def build_p12_test():
    nc = bass.Bass("TRN2", target_bir_lowering=False, debug=False)
    g = {}
    for k in ("w_inT", "convw_col", "convb_col", "alog_col", "d_col", "w_xprojT",
              "w_dtT", "bdt_col", "w_outT", "lng_col", "lnb_col"):
        g[k] = nc.dram_tensor(k, list(WEIGHT_SHAPES[k]), F32, kind="ExternalInput").ap()
    xT = nc.dram_tensor("xT", [D, T], F32, kind="ExternalInput").ap()
    x1T = nc.dram_tensor("x1T", [D, T], F32, kind="ExternalOutput").ap()
    scr = dict(
        xsp=nc.dram_tensor("scr_xsp", [DI, T], F32, kind="Internal").ap(),
        sz=nc.dram_tensor("scr_sz", [DI, T], F32, kind="Internal").ap(),
        l=nc.dram_tensor("scr_l", [DI, T], F32, kind="Internal").ap(),
        B=nc.dram_tensor("scr_B", [16, T], F32, kind="Internal").ap(),
        C=nc.dram_tensor("scr_C", [16, T], F32, kind="Internal").ap(),
    )
    ctx = Ctx(nc)
    sh = emit_shared(ctx)
    ctx.push()
    emit_p1(ctx, g, xT, scr, sh)
    ctx.full_barrier()
    ctx.pop()
    emit_p2(ctx, g, xT, scr, x1T, sh)
    ctx.full_barrier()
    ctx.build()
    return nc


_NC_CACHE = {}


def _get_ncs():
    if "ncs" not in _NC_CACHE:
        _NC_CACHE["ncs"] = (build_p12_test(), build_p3_test(), build_p4_test(),
                            build_p5_test())
    return _NC_CACHE["ncs"]


P12_W = ("w_inT", "convw_col", "convb_col", "alog_col", "d_col", "w_xprojT",
         "w_dtT", "bdt_col", "w_outT", "lng_col", "lnb_col")
P3_W = ("w_fcT", "w_projT", "vT", "fbT", "uT", "gate")
P4_W = ("w_qkv", "bqkv_col", "selz", "eye128")
P5_W = ("u1", "v1", "b1_col", "u2", "v2", "b2_col")


def kernel_unfused(**inputs):
    g = host_layouts(inputs)
    x = np.ascontiguousarray(np.asarray(inputs["L0_attn_x"], np.float32))
    nc12, nc3, nc4, nc5 = _get_ncs()
    cores = list(range(NCORE))

    ims = [{k: g[k] for k in P12_W} for _ in cores]
    for b in cores:
        ims[b]["xT"] = np.ascontiguousarray(x[b].T)
    r1 = run_bass_kernel_spmd(nc12, ims, core_ids=cores)
    x1 = [r["x1T"] for r in r1.results]

    ims = [{k: g[k] for k in P3_W} for _ in cores]
    for b in cores:
        ims[b]["x1T"] = np.ascontiguousarray(x1[b])
    r2 = run_bass_kernel_spmd(nc3, ims, core_ids=cores)
    x2 = [r["x2T"] for r in r2.results]

    ims = [{k: g[k] for k in P4_W} for _ in cores]
    for b in cores:
        ims[b]["x3T"] = np.ascontiguousarray(x2[b])
    r3 = run_bass_kernel_spmd(nc4, ims, core_ids=cores)
    x3 = [r["x4T"] for r in r3.results]

    ims = [{k: g[k] for k in P5_W} for _ in cores]
    for b in cores:
        ims[b]["x4T"] = np.ascontiguousarray(x3[b])
    r4 = run_bass_kernel_spmd(nc5, ims, core_ids=cores)
    out = np.stack([np.ascontiguousarray(r["outT"].T) for r in r4.results])
    return out.astype(np.float32)


ALL_W = tuple(set(P12_W) | set(P3_W) | set(P4_W) | set(P5_W))


FUSED_DBG = False


def build_fused():
    nc = bass.Bass("TRN2", target_bir_lowering=False, debug=False)
    g = {}
    for k in ALL_W:
        g[k] = nc.dram_tensor(k, list(WEIGHT_SHAPES[k]), F32, kind="ExternalInput").ap()
    xT = nc.dram_tensor("xT", [D, T], F32, kind="ExternalInput").ap()
    outT = nc.dram_tensor("outT", [D, T], F32, kind="ExternalOutput").ap()
    ikind = "ExternalOutput" if FUSED_DBG else "Internal"
    x1T = nc.dram_tensor("x1T", [D, T], F32, kind=ikind).ap()
    x2T = nc.dram_tensor("x2T", [D, T], F32, kind=ikind).ap()
    x3T = nc.dram_tensor("x3T", [D, T], F32, kind=ikind).ap()
    scr = dict(
        xsp=nc.dram_tensor("scr_xsp", [DI, T], F32, kind="Internal").ap(),
        sz=nc.dram_tensor("scr_sz", [DI, T], F32, kind="Internal").ap(),
        l=nc.dram_tensor("scr_l", [DI, T], F32, kind="Internal").ap(),
        B=nc.dram_tensor("scr_B", [16, T], F32, kind="Internal").ap(),
        C=nc.dram_tensor("scr_C", [16, T], F32, kind="Internal").ap(),
    )
    ctx = Ctx(nc)
    sh = emit_shared(ctx)
    ctx.push()
    emit_p1(ctx, g, xT, scr, sh)
    ctx.full_barrier()
    ctx.pop()
    ctx.push()
    emit_p2(ctx, g, xT, scr, x1T, sh)
    ctx.full_barrier()
    ctx.pop()
    ctx.push()
    emit_p3(ctx, g, x1T, x2T, sh)
    ctx.full_barrier()
    ctx.pop()
    ctx.push()
    emit_p4(ctx, g, x2T, x3T, sh)
    ctx.full_barrier()
    ctx.pop()
    dump_w = None
    if FUSED_DBG:
        dump_w = dict(
            u1=nc.dram_tensor("dw_u1", [128, 8 * 256], BF16, kind="ExternalOutput").ap(),
            v1=nc.dram_tensor("dw_v1", [128, 32 * 256], BF16, kind="ExternalOutput").ap(),
            u2=nc.dram_tensor("dw_u2", [128, 32 * 256], BF16, kind="ExternalOutput").ap(),
            v2=nc.dram_tensor("dw_v2", [128, 8 * 256], BF16, kind="ExternalOutput").ap(),
            b1=nc.dram_tensor("dw_b1", [128, 32], F32, kind="ExternalOutput").ap(),
            b2=nc.dram_tensor("dw_b2", [128, 8], F32, kind="ExternalOutput").ap(),
        )
    og = emit_p5(ctx, g, x3T, outT, sh, dump_w=dump_w)
    ctx.need("sync", *og[-1])
    ctx.full_barrier(engs=("sync",))
    ctx.build()
    return nc


def kernel(**inputs):
    _ = inputs.get("L1_attn_W_proj")
    _ = inputs.get("L1_attn_b_proj")
    g = host_layouts(inputs)
    x = np.ascontiguousarray(np.asarray(inputs["L0_attn_x"], np.float32))
    if "fused" not in _NC_CACHE:
        _NC_CACHE["fused"] = build_fused()
    nc = _NC_CACHE["fused"]
    cores = list(range(NCORE))
    ims = [{k: g[k] for k in ALL_W} for _ in cores]
    for b in cores:
        ims[b]["xT"] = np.ascontiguousarray(x[b].T)
    r = run_bass_kernel_spmd(nc, ims, core_ids=cores)
    out = np.stack([np.ascontiguousarray(q["outT"].T) for q in r.results])
    return out.astype(np.float32)


# version 08 (final)
# speedup vs baseline: 1.0358x; 1.0358x over previous
import sys
import numpy as np

sys.path.insert(0, "/opt/trn_rl_repo")

import concourse.bass as bass
import concourse.mybir as mybir
from concourse.bass_utils import run_bass_kernel_spmd

F32 = mybir.dt.float32
BF16 = mybir.dt.bfloat16
AF = mybir.ActivationFunctionType
ALU = mybir.AluOpType
AX = mybir.AxisListType

NCORE = 8
T = 2048
D = 1024
DI = 2048
NST = 16
TC = 512
NCH = T // TC
NOLN = False

ENGS = ("pe", "act", "dve", "pool", "sync")
REDIR = True


class Action:
    __slots__ = ("eng", "fn", "marked", "is_dma", "label", "waited")

    def __init__(self, eng, fn, is_dma=False, label=""):
        self.eng = eng
        self.fn = fn
        self.marked = False
        self.is_dma = is_dma
        self.label = label
        self.waited = False


class Ctx:
    def __init__(self, nc):
        self.nc = nc
        self.acts = {e: [] for e in ENGS}
        self._cm = []
        self._marks = []
        self.last_act = {e: None for e in ENGS}

    def push(self):
        self._marks.append(len(self._cm))

    def pop(self):
        m = self._marks.pop()
        for cm in reversed(self._cm[m:]):
            cm.__exit__(None, None, None)
        del self._cm[m:]

    def sb(self, name, shape, dtype):
        cm = self.nc.sbuf_tensor(name, list(shape), dtype)
        t = cm.__enter__()
        self._cm.append(cm)
        return t

    def ps(self, name, shape, dtype=F32):
        cm = self.nc.psum_tensor(name, list(shape), dtype)
        t = cm.__enter__()
        self._cm.append(cm)
        return t

    def on(self, eng, fn, is_dma=False, label=""):
        a = Action(eng, fn, is_dma, label)
        self.acts[eng].append(a)
        self.last_act[eng] = a
        return a

    def full_barrier(self, engs=ENGS):
        for e in engs:
            for o in engs:
                if o != e and self.last_act.get(o) is not None:
                    self.need(e, self.last_act[o])

    def dma(self, eng, out, in_, **kw):
        return self.on(eng,
                       lambda E, out=out, in_=in_, kw=kw: E.dma_start(out=out, in_=in_, **kw),
                       is_dma=True)

    def need(self, eng, *producers):
        for p in producers:
            if p is None or (isinstance(p, Action) and p.eng == eng):
                continue
            if isinstance(p, (list, tuple)):
                self.need(eng, *p)
                continue
            p.marked = True
            p.waited = True
            self.acts[eng].append(("wait", p))

    def barrier(self, producers, engs=ENGS):
        for e in engs:
            self.need(e, *producers)

    def build(self):
        nc = self.nc
        sems = {}
        for e in ENGS:
            cm = nc.semaphore(f"sem_{e}")
            sems[e] = cm.__enter__()
            self._cm.append(cm)
        val = {}
        for e in ENGS:
            c = 0
            for a in self.acts[e]:
                if isinstance(a, Action) and a.is_dma:
                    a.marked = True
                if isinstance(a, Action) and a.marked:
                    c += 16 if a.is_dma else 1
                    val[id(a)] = c
        sync_total = val.get(id(next(
            (a for a in reversed(self.acts["sync"]) if isinstance(a, Action)), None)), 0)

        def body(E, ename):
            E.sem_clear(sems[ename])
            waited = {}
            for a in self.acts[ename]:
                if isinstance(a, tuple):
                    p = a[1]
                    v = val[id(p)]
                    if waited.get(p.eng, 0) >= v:
                        continue
                    waited[p.eng] = v
                    E.wait_ge(sems[p.eng], v)
                else:
                    inst = a.fn(E)
                    if a.marked:
                        inst.then_inc(sems[a.eng], 16 if a.is_dma else 1)
            if ename == "sync" and sync_total:
                E.wait_ge(sems["sync"], sync_total)

        blk = nc.Block()
        block = blk.__enter__()
        block.tensor(lambda E: body(E, "pe"))
        block.scalar(lambda E: body(E, "act"))
        block.vector(lambda E: body(E, "dve"))
        block.gpsimd(lambda E: body(E, "pool"))
        block.sync(lambda E: body(E, "sync"))
        blk.__exit__(None, None, None)
        for cm in reversed(self._cm):
            cm.__exit__(None, None, None)


def col(tile, j):
    return tile[:, j:j + 1]


def host_layouts(inp):
    f = np.ascontiguousarray
    g = {}
    g["w_inT"] = f(inp["L0_attn_W_in"].T)
    g["convw_col"] = f(inp["L0_attn_conv_w"].reshape(DI, 4)
                       .reshape(16, 128, 4).transpose(1, 0, 2).reshape(128, 64))
    g["convb_col"] = f(inp["L0_attn_conv_b"].reshape(16, 128).T)
    g["alog_col"] = f(inp["L0_attn_A_log"]
                      .reshape(16, 128, NST).transpose(1, 0, 2).reshape(128, 256))
    g["d_col"] = f(inp["L0_attn_D_param"].reshape(16, 128).T)
    g["w_xprojT"] = f(inp["L0_attn_W_xproj"].T)
    g["w_dtT"] = f(inp["L0_attn_W_dt"].T)
    g["bdt_col"] = f(inp["L0_attn_b_dt"].reshape(16, 128).T)
    g["w_outT"] = f(inp["L0_attn_W_out"].T)
    g["lng_col"] = f(inp["L0_attn_ln_gamma"].reshape(8, 128).T)
    g["lnb_col"] = f(inp["L0_attn_ln_beta"].reshape(8, 128).T)
    g["w_fcT"] = f(inp["L0_ffn_w_fc"].T)
    g["w_projT"] = f(inp["L0_ffn_w_proj"].T)
    g["vT"] = f(inp["L0_ffn_v"].T)
    g["fbT"] = f(inp["L0_ffn_fast_b"].T)
    g["uT"] = f(inp["L0_ffn_u"].T)
    g["gate"] = f(np.broadcast_to(np.asarray(inp["L0_ffn_gate"], np.float32).reshape(1, 1), (128, 1)).copy())
    g["w_qkv"] = f(inp["L1_attn_W_qkv"])
    g["bqkv_col"] = f(inp["L1_attn_b_qkv"].reshape(24, 128).T)
    selz = np.zeros((16, 1024), np.float32)
    for hp in range(8):
        selz[2 * hp, hp * 128:hp * 128 + 64] = 1.0
        selz[2 * hp + 1, hp * 128 + 64:hp * 128 + 128] = 1.0
    g["selz"] = selz
    g["eye128"] = np.eye(128, dtype=np.float32)
    g["u1"] = f(inp["L1_ffn_U1"])
    g["v1"] = f(inp["L1_ffn_V1"])
    g["b1_col"] = f(inp["L1_ffn_b1"].reshape(32, 128).T)
    g["u2"] = f(inp["L1_ffn_U2"])
    g["v2"] = f(inp["L1_ffn_V2"])
    g["b2_col"] = f(inp["L1_ffn_b2"].reshape(8, 128).T)
    return g


WEIGHT_SHAPES = {
    "w_inT": (D, 4096), "convw_col": (128, 64), "convb_col": (128, 16),
    "alog_col": (128, 256), "d_col": (128, 16), "w_xprojT": (DI, 33),
    "w_dtT": (33, DI), "bdt_col": (128, 16), "w_outT": (DI, D),
    "lng_col": (128, 8), "lnb_col": (128, 8),
    "w_fcT": (D, 4096), "w_projT": (4096, D), "vT": (D, 16), "fbT": (16, 16),
    "uT": (16, 4096), "gate": (128, 1),
    "w_qkv": (D, 3072), "bqkv_col": (128, 24),
    "selz": (16, 1024), "eye128": (128, 128),
    "u1": (D, 256), "v1": (256, 4096), "b1_col": (128, 32),
    "u2": (4096, 256), "v2": (256, 1024), "b2_col": (128, 8),
}


class WTiles:

    def __init__(self, tiles, mw):
        self.tiles = tiles
        self.mw = mw

    def mm(self, k, m):
        j, off = divmod(m * 128, self.mw)
        t = self.tiles[k][j]
        return t[:, off:off + 128]

    def full(self, k, j=0):
        return self.tiles[k][j][:, :]


def load_bf16_tiles(ctx, name, hbm, R, C, stages, guard=()):
    ntile = (R + 127) // 128
    mw = min(C, 256)
    nj = (C + mw - 1) // mw
    tiles, ready = [], []
    prev_cast = [list(guard), list(guard)]
    lds, rows, lens = [], [], []

    def emit_cast(i, lag_ld):
        n = lens[i]
        stg = stages[i % 2]
        ctx.need("act", lds[i], lag_ld)
        cs = None
        for j in range(nj):
            cs = ctx.on("act", lambda E, t=rows[i][j], s=stg, n=n, j=j, mw=mw:
                        E.activation(out=t[:, :], in_=s[0:n, j * mw:(j + 1) * mw],
                                     func=AF.Copy))
        prev_cast[i % 2] = [cs]
        ready.append(cs)

    for i in range(ntile):
        r0, r1 = i * 128, min(R, (i + 1) * 128)
        rows.append([ctx.sb(f"{name}_b{i}_{j}", [r1 - r0, mw], BF16) for j in range(nj)])
        lens.append(r1 - r0)
        tiles.append(rows[i])
        if prev_cast[i % 2]:
            ctx.need("sync", *prev_cast[i % 2])
        lds.append(ctx.dma("sync", stages[i % 2][0:r1 - r0, 0:C], hbm[r0:r1, :]))
        if i >= 1:
            emit_cast(i - 1, lds[i])
    s2 = ntile % 2
    if prev_cast[s2]:
        ctx.need("sync", *prev_cast[s2])
    r0 = (ntile - 1) * 128
    dummy = ctx.dma("sync", stages[s2][0:lens[-1], 0:C], hbm[r0:r0 + lens[-1], :])
    emit_cast(ntile - 1, dummy)
    return WTiles(tiles, mw), ready


def emit_ln(ctx, sh, x_tiles, tc, ones_col, ones_row, ps_stat, ps_b, sq_stage,
            pfx, out_dtype=BF16, gamma=None, beta=None, combined=False,
            extra_need=(), psb_guard=(), bufs=None, xn_guard=()):
    eps = sh["eps_ap"]
    """Plain (or combined-double) LayerNorm over feature axis, feature-major.
    x_tiles: 8 f32 tiles [128, tc]. Returns (xn_tiles(out_dtype), last_actions).
    combined=True: LN then LN-with-gamma/beta fused analytically (var2 = var/(var+eps))."""
    nd = len(x_tiles)
    acts_sq = []
    mm = []
    for i, xt in enumerate(x_tiles):
        ctx.need("act", *extra_need)
        if i >= 2:
            ctx.need("act", mm[2 * (i - 2) + 1])
        a = ctx.on("act", lambda E, s=sq_stage[i % 2], x=xt: E.activation(
            out=s[:, 0:tc], in_=x[:, 0:tc], func=AF.Square))
        acts_sq.append(a)
        ctx.need("pe", *extra_need)
        mm.append(ctx.on("pe", lambda E, x=xt, i=i: E.matmul(
            ps_stat[0:1, 0:tc], ones_col[:, :], x[:, 0:tc],
            start=(i == 0), stop=(i == nd - 1))))
        ctx.need("pe", acts_sq[i])
        mm.append(ctx.on("pe", lambda E, s=sq_stage[i % 2], i=i: E.matmul(
            ps_stat[32:33, 0:tc], ones_col[:, :], s[:, 0:tc],
            start=(i == 0), stop=(i == nd - 1))))
    rows = bufs["rows"]
    mu, e2, var, srow = (rows[0:1, 0:tc], rows[0:1, tc:2 * tc],
                         rows[0:1, 2 * tc:3 * tc], rows[0:1, 3 * tc:4 * tc])
    v2r = rows[0:1, 4 * tc:5 * tc] if combined else None
    ctx.need("dve", mm[-2], mm[-1])
    ctx.on("dve", lambda E: E.tensor_scalar(out=mu, in0=ps_stat[0:1, 0:tc],
                                            scalar1=1.0 / D, scalar2=None, op0=ALU.mult))
    ctx.on("dve", lambda E: E.tensor_scalar(out=e2, in0=ps_stat[32:33, 0:tc],
                                            scalar1=1.0 / D, scalar2=None, op0=ALU.mult))
    ctx.on("dve", lambda E: E.tensor_tensor(out=var, in0=mu, in1=mu, op=ALU.mult))
    dv = ctx.on("dve", lambda E: E.tensor_tensor(out=var, in0=e2, in1=var, op=ALU.subtract))
    ctx.need("act", dv)
    a1 = ctx.on("act", lambda E: E.activation(out=srow, in_=var, func=AF.Sqrt, bias=eps))
    ctx.need("dve", a1)
    r1 = ctx.on("dve", lambda E: E.reciprocal(out=srow, in_=srow))
    if combined:
        ctx.on("dve", lambda E: E.tensor_tensor(out=v2r, in0=srow, in1=srow, op=ALU.mult))
        d2 = ctx.on("dve", lambda E: E.tensor_tensor(out=v2r, in0=var, in1=v2r, op=ALU.mult))
        ctx.need("act", d2)
        a2 = ctx.on("act", lambda E: E.activation(out=v2r, in_=v2r, func=AF.Sqrt, bias=eps))
        ctx.need("dve", a2)
        ctx.on("dve", lambda E: E.reciprocal(out=v2r, in_=v2r))
        r1 = ctx.on("dve", lambda E: E.tensor_tensor(out=srow, in0=srow, in1=v2r, op=ALU.mult))
    ctx.need("pe", r1, *psb_guard)
    b1 = ctx.on("pe", lambda E: E.matmul(ps_b[:, 0:tc], ones_row[:, :], mu,
                                         start=True, stop=True))
    b2 = ctx.on("pe", lambda E: E.matmul(ps_b[:, 512:512 + tc], ones_row[:, :], srow,
                                         start=True, stop=True))
    mub = bufs["mub"]
    ctx.need("act", b1, b2)
    ctx.on("act", lambda E: E.activation(out=mub[:, 0:tc], in_=ps_b[:, 0:tc],
                                         func=AF.Copy), label=pfx + "mubcp1")
    ev = ctx.on("act", lambda E: E.activation(out=mub[:, tc:2 * tc],
                                              in_=ps_b[:, 512:512 + tc], func=AF.Copy),
                label=pfx + "mubcp2")
    xn_tiles, outs = [], []
    for i, xt in enumerate(x_tiles):
        xn = bufs["xn"][i]
        tmp = sq_stage[i % 2]
        ctx.need("dve", ev, *xn_guard)
        ctx.on("dve", lambda E, x=xt, t=tmp: E.tensor_tensor(
            out=t[:, 0:tc], in0=x[:, 0:tc], in1=mub[:, 0:tc], op=ALU.subtract))
        if gamma is not None:
            ctx.on("dve", lambda E, t=tmp: E.tensor_tensor(
                out=t[:, 0:tc], in0=t[:, 0:tc], in1=mub[:, tc:2 * tc], op=ALU.mult))
            ctx.on("dve", lambda E, t=tmp, i=i: E.tensor_scalar(
                out=t[:, 0:tc], in0=t[:, 0:tc], scalar1=col(gamma, i), scalar2=col(beta, i),
                op0=ALU.mult, op1=ALU.add))
        else:
            ctx.on("dve", lambda E, t=tmp: E.tensor_tensor(
                out=t[:, 0:tc], in0=t[:, 0:tc], in1=mub[:, tc:2 * tc], op=ALU.mult),
                   label=f"{pfx}norm{i}")
        o = ctx.on("dve", lambda E, t=tmp, x2=xn: E.tensor_copy(
            out=x2[:, :], in_=t[:, 0:tc]), label=f"{pfx}cast{i}")
        xn_tiles.append(xn)
        outs.append(o)
    ctx.need("act", *outs)
    hop = ctx.on("act", lambda E: E.activation(out=rows[0:1, 2 * tc:2 * tc + 1],
                                               in_=rows[0:1, 0:1], func=AF.Copy))
    return xn_tiles, [hop] * len(outs)


P5BARS = set()


def emit_p5(ctx, g, x4T, outT, sh, xin_guard=None, dbg=None, dump_w=None):
    sb, ps, on, dma, need = ctx.sb, ctx.ps, ctx.on, ctx.dma, ctx.need
    bar = ctx.full_barrier
    stg = [sb("p5stg0", [128, 4096], F32), sb("p5stg1", [128, 4096], F32)]
    u1b, r1 = load_bf16_tiles(ctx, "u1", g["u1"], D, 256, stg)
    v1b, r2 = load_bf16_tiles(ctx, "v1", g["v1"], 256, 4096, stg, guard=r1[-2:])
    u2b, r3 = load_bf16_tiles(ctx, "u2", g["u2"], 4096, 256, stg, guard=r2[-2:])
    v2b, r4 = load_bf16_tiles(ctx, "v2", g["v2"], 256, 1024, stg, guard=r3[-2:])
    ctx.need("pe", r1[-1], r2[-1], r3[-1], r4[-1])
    if "w" in P5BARS:
        bar()
    b1c = sb("p5b1c", [128, 32], F32)
    ld_b1 = dma("sync", b1c[:, :], g["b1_col"][:, :])
    b2c = sb("p5b2c", [128, 8], F32)
    ld_b2 = dma("sync", b2c[:, :], g["b2_col"][:, :])

    ps_stat = ps("p5stat", [33, 512])
    ps_P = ps("p5P", [128, 1024])
    ps_b = None
    ps_H = ps("p5H", [128, 1024])
    ps_Q = ps("p5Q", [128, 1024])
    ps_b = ps_Q
    ps_O = ps_H

    x4sb = [[sb(f"p5x4_{s}_{i}", [128, TC], F32) for i in range(8)] for s in range(2)]
    sqst = [[sb(f"p5sq_{s}_{i}", [128, TC], F32) for i in range(2)] for s in range(2)]
    lnbufs = [dict(rows=sb(f"p5rows{s}", [1, 5 * TC], F32),
                   mub=sb(f"p5mub{s}", [128, 2 * TC], F32),
                   xn=[sb(f"p5xn_{s}_{i}", [128, TC], BF16) for i in range(8)])
              for s in range(2)]
    Pb = [sb(f"p5Pb{i}", [128, TC], BF16) for i in range(2)]
    Hb = [sb(f"p5Hb{i}", [128, TC], BF16) for i in range(2)]
    Qb = [sb(f"p5Qb{i}", [128, TC], BF16) for i in range(2)]
    ot = [sb(f"p5ot{i}", [128, TC], F32) for i in range(4)]
    slot_guard = [[], []]
    out_guard = [[] for _ in range(NCH)]
    ctx.need("pe", *sh["ones_ready"])
    prev_evP = []
    prev_evQ = []
    prev_last_oev = None
    xn_guard = [[], []]
    ot_store = [None] * 4

    for c in range(NCH):
        s = c % 2
        t0 = c * TC
        lds = []
        for i in range(8):
            if xin_guard is not None:
                need("sync", xin_guard[c])
            need("sync", *slot_guard[s])
            lds.append(dma("sync", x4sb[s][i][:, :], x4T[i * 128:(i + 1) * 128, t0:t0 + TC]))
        lds.append(dma("sync", sh["lag"][:, :], x4T[896:1024, t0:t0 + 128]))
        slot_guard[s] = []
        if "a" in P5BARS:
            bar()
        guards = slot_guard[s] + ([] if c < 2 else [])
        if NOLN:
            xnb = lnbufs[c]["xn"]
            xno = []
            for i in range(8):
                need("dve", lds[i], *xn_guard[s])
                xno.append(on("dve", lambda E, i=i, x2=xnb[i], x=x4sb[s][i]: E.tensor_copy(
                    out=x2[:, :], in_=x[:, :])))
        else:
            xnb, xno = emit_ln(ctx, sh, x4sb[s], TC, sh["ones_col"], sh["ones_row"],
                               ps_stat, ps_b, sqst[s], f"p5c{c}_", out_dtype=BF16,
                               extra_need=lds, psb_guard=prev_evQ, bufs=lnbufs[s],
                               xn_guard=xn_guard[s])
        xn_guard[s] = []
        dumps = []
        if dbg is not None and c in (0, 1):
            for i in range(8):
                need("sync", xno[i])
                dumps.append(dma("sync", dbg[f"xn{c}"][i * 128:(i + 1) * 128, :], xnb[i][:, :]))
            need("pe", *dumps)
        dbgP = dbg is not None and c in (0, 1)
        if "b" in P5BARS:
            bar()
        evP = []
        prev_evP = evP
        for m in range(2):
            for k in range(8):
                need("pe", xno[k])
                pe_done = on("pe", lambda E, m=m, k=k, xnb=xnb: E.matmul(
                    ps_P[:, m * 512:m * 512 + TC], u1b.mm(k, m),
                    xnb[k][:, :], start=(k == 0), stop=(k == 7)))
                if m == 1:
                    xn_guard[s].append(pe_done)
            need("act", pe_done, ld_b1)
            evP.append(on("act", lambda E, m=m: E.activation(
                out=Pb[m][:, :], in_=ps_P[:, m * 512:m * 512 + TC], func=AF.Copy)))
            if dbgP:
                need("sync", evP[-1])
                dma("sync", dbg[f"P{c}"][m * 128:(m + 1) * 128, :], Pb[m][:, :])
        if "c" in P5BARS:
            bar()
        evH_prev = [None, None]
        for m in range(32):
            hb = Hb[m % 2]
            for k in range(2):
                need("pe", evP[k])
                if m == 0 and prev_last_oev is not None:
                    need("pe", prev_last_oev)
                if evH_prev[m % 2] is not None:
                    need("pe", evH_prev[m % 2][1])
                hdone = on("pe", lambda E, m=m, k=k: E.matmul(
                    ps_H[:, (m % 2) * 512:(m % 2) * 512 + TC],
                    v1b.mm(k, m), Pb[k][:, :],
                    start=(k == 0), stop=(k == 1)))
            need("act", hdone)
            if evH_prev[m % 2] is not None:
                need("act", evH_prev[m % 2][2])
            ev = on("act", lambda E, hb=hb, m=m: E.activation(
                out=hb[:, :], in_=ps_H[:, (m % 2) * 512:(m % 2) * 512 + TC],
                func=AF.Gelu_apprx_tanh, bias=col(b1c, m)))
            qmm = None
            for m2 in range(2):
                need("pe", ev)
                qmm = on("pe", lambda E, m2=m2, m=m, hb=hb: E.matmul(
                    ps_Q[:, m2 * 512:m2 * 512 + TC], u2b.mm(m, m2),
                    hb[:, :], start=(m == 0), stop=(m == 31)))
            evH_prev[m % 2] = (None, ev, qmm)
        qdone = qmm
        evQ = []
        prev_evQ = evQ
        for m2 in range(2):
            need("act", qdone)
            evQ.append(on("act", lambda E, m2=m2: E.activation(
                out=Qb[m2][:, :], in_=ps_Q[:, m2 * 512:m2 * 512 + TC], func=AF.Copy)))
        if "d" in P5BARS:
            bar()
        prev_oev = None
        for m in range(8):
            for k in range(2):
                need("pe", evQ[k])
                if prev_oev is not None:
                    need("pe", prev_oev)
                odone = on("pe", lambda E, m=m, k=k: E.matmul(
                    ps_O[:, 0:TC], v2b.mm(k, m), Qb[k][:, :],
                    start=(k == 0), stop=(k == 1)))
            o = ot[m % 4]
            need("dve", odone, ld_b2)
            need("dve", ot_store[m % 4], ot_store[(m + 1) % 4])
            oev = on("dve", lambda E, o=o, m=m: E.tensor_scalar(
                out=o[:, :], in0=ps_O[:, 0:TC], scalar1=col(b2c, m), scalar2=None,
                op0=ALU.add))
            prev_oev = oev
            add = on("dve", lambda E, o=o, x4t=x4sb[s][m]: E.tensor_tensor(
                out=o[:, :], in0=o[:, :], in1=x4t[:, :], op=ALU.add))
            need("sync", add)
            st = dma("sync", outT[m * 128:(m + 1) * 128, t0:t0 + TC], o[:, :])
            ot_store[m % 4] = st
            out_guard[c].append(st)
            slot_guard[s].append(add)
        if "e" in P5BARS:
            bar()
        prev_last_oev = prev_oev
        slot_guard[s] += [qdone]
        if c == 0 and dbg is not None and "Plate" in dbg:
            first_xnb = xnb
    if dump_w is not None:
        need("sync", prev_last_oev, qdone)
        for nm, wt in (("u1", u1b), ("v1", v1b), ("u2", u2b), ("v2", v2b)):
            k = 0
            for row in wt.tiles:
                for t in row:
                    dma("sync", dump_w[nm][0:t.shape[0], k * 256:k * 256 + 256], t[:, :])
                    k += 1
        dma("sync", dump_w["b1"][:, :], b1c[:, :])
        dma("sync", dump_w["b2"][:, :], b2c[:, :])
    if dbg is not None and "Plate" in dbg:
        need("pe", prev_last_oev)
        for k in range(8):
            lmm = on("pe", lambda E, k=k: E.matmul(
                ps_P[:, 0:TC], u1b[k][:, 0:128], first_xnb[k][:, :],
                start=(k == 0), stop=(k == 7)))
        need("act", lmm)
        lev = on("act", lambda E: E.activation(out=Pb[0][:, :], in_=ps_P[:, 0:TC],
                                               func=AF.Copy))
        need("sync", lev)
        dma("sync", dbg["Plate"][:, :], Pb[0][:, :])
    return out_guard


def emit_shared(ctx):
    sh = {}
    sh["ones_col"] = ctx.sb("ones_col", [128, 1], F32)
    sh["ones_row"] = ctx.sb("ones_row", [1, 128], F32)
    cst = ctx.sb("consts", [128, 4], F32)
    a = ctx.on("dve", lambda E: E.memset(sh["ones_col"][:, :], 1.0))
    b = ctx.on("dve", lambda E: E.memset(sh["ones_row"][:, :], 1.0))
    c0 = ctx.on("dve", lambda E: E.memset(cst[:, 0:1], 1e-5))
    c1 = ctx.on("dve", lambda E: E.memset(cst[:, 1:2], 1.0))
    c2 = ctx.on("dve", lambda E: E.memset(cst[:, 2:3], 1e-6))
    c3 = ctx.on("dve", lambda E: E.memset(cst[:, 3:4], 0.001))
    sh["cst"] = cst
    sh["lag"] = ctx.sb("lag_tile", [128, 128], F32)
    sh["eps_ap"] = cst[0:1, 0:1]
    sh["ones_ready"] = [a, b, c0, c1, c2, c3]
    return sh


def build_p5_test(debug=False):
    nc = bass.Bass("TRN2", target_bir_lowering=False, debug=False)
    g = {}
    for k in ("u1", "v1", "b1_col", "u2", "v2", "b2_col"):
        g[k] = nc.dram_tensor(k, list(WEIGHT_SHAPES[k]), F32, kind="ExternalInput").ap()
    x4T = nc.dram_tensor("x4T", [D, T], F32, kind="ExternalInput").ap()
    outT = nc.dram_tensor("outT", [D, T], F32, kind="ExternalOutput").ap()
    dbg = None
    if debug:
        dbg = {f"xn{c}": nc.dram_tensor(f"dbg_xn{c}", [D, TC], BF16,
                                        kind="ExternalOutput").ap() for c in (0, 1)}
        dbg.update({f"P{c}": nc.dram_tensor(f"dbg_P{c}", [256, TC], BF16,
                                            kind="ExternalOutput").ap() for c in (0, 1)})
        dbg["Plate"] = nc.dram_tensor("dbg_Plate", [128, TC], BF16,
                                      kind="ExternalOutput").ap()
    ctx = Ctx(nc)
    sh = emit_shared(ctx)
    og = emit_p5(ctx, g, x4T, outT, sh, dbg=dbg)
    ctx.need("sync", *og[-1])
    ctx.build()
    return nc



MAGIC = 12582912.0


def emit_bitlinear_prep(ctx, g, key, R, C, sh, stg, pfx, guard=()):
    sb, on, dma, need = ctx.sb, ctx.on, ctx.dma, ctx.need
    bar = ctx.full_barrier
    nk = R // 128
    ncc = C // 512
    mw = min(C, 256)
    nj = C // mw
    tiles = [[sb(f"{pfx}we{i}_{j}", [128, mw], BF16) for j in range(nj)]
             for i in range(nk)]
    ctx.push()
    ps_m = ctx.ps(pfx + "psm", [128, 1024])
    rowbuf = sb(pfx + "rows", [1, 2 * C], F32)
    scl = rowbuf[0:1, 0:C]
    inv = rowbuf[0:1, C:2 * C]
    sclb = sb(pfx + "sclb", [128, C], F32)
    invb = sb(pfx + "invb", [128, C], F32)
    trscr = sb(pfx + "trscr", [128, C], F32)
    need("sync", *guard)
    for i in range(nk):
        s = stg[0]
        bar()
        dma("sync", s[:, 0:C], g[key][i * 128:(i + 1) * 128, :])
        bar()
        on("act", lambda E, s=s, C=C: E.activation(out=s[:, 0:C], in_=s[:, 0:C],
                                                   func=AF.Abs))
        bar()
        for cc in range(ncc):
            on("pe", lambda E, s=s, cc=cc: E.matmul(
                ps_m[0:1, (cc % 2) * 512:(cc % 2) * 512 + 512], sh["ones_col"][:, :],
                s[:, cc * 512:cc * 512 + 512], start=True, stop=True))
            bar()
            if i == 0:
                on("dve", lambda E, cc=cc: E.tensor_copy(
                    out=scl[0:1, cc * 512:cc * 512 + 512],
                    in_=ps_m[0:1, (cc % 2) * 512:(cc % 2) * 512 + 512]))
            else:
                on("dve", lambda E, cc=cc: E.tensor_tensor(
                    out=scl[0:1, cc * 512:cc * 512 + 512],
                    in0=scl[0:1, cc * 512:cc * 512 + 512],
                    in1=ps_m[0:1, (cc % 2) * 512:(cc % 2) * 512 + 512], op=ALU.add))
            bar()
    on("dve", lambda E: E.tensor_scalar(out=scl, in0=scl, scalar1=1.0 / R,
                                        scalar2=1e-5, op0=ALU.mult, op1=ALU.max))
    bar()
    on("dve", lambda E: E.reciprocal(out=inv, in_=scl))
    bar()
    for cc in range(ncc):
        for j2, (row, dstb) in enumerate(((scl, sclb), (inv, invb))):
            on("pe", lambda E, row=row, cc=cc, j2=j2: E.matmul(
                ps_m[0:128, j2 * 512:j2 * 512 + 512], sh["ones_row"][:, :],
                row[0:1, cc * 512:cc * 512 + 512], start=True, stop=True))
            bar()
            on("act", lambda E, dstb=dstb, cc=cc, j2=j2: E.activation(
                out=dstb[:, cc * 512:cc * 512 + 512],
                in_=ps_m[0:128, j2 * 512:j2 * 512 + 512], func=AF.Copy))
            bar()
    ready = []
    for i in range(nk):
        s = stg[0]
        bar()
        dma("sync", s[:, 0:C], g[key][i * 128:(i + 1) * 128, :])
        bar()
        on("dve", lambda E, s=s, C=C: E.tensor_tensor(
            out=trscr[:, 0:C], in0=s[:, 0:C], in1=invb[:, :], op=ALU.mult))
        bar()
        on("dve", lambda E, s=s, C=C: E.tensor_scalar(
            out=s[:, 0:C], in0=trscr[:, 0:C], scalar1=MAGIC, scalar2=None, op0=ALU.add))
        bar()
        on("dve", lambda E, s=s, C=C: E.tensor_scalar(
            out=trscr[:, 0:C], in0=s[:, 0:C], scalar1=-MAGIC, scalar2=None, op0=ALU.add))
        bar()
        on("dve", lambda E, s=s, C=C: E.tensor_scalar(
            out=s[:, 0:C], in0=trscr[:, 0:C], scalar1=1.0, scalar2=-1.0,
            op0=ALU.min, op1=ALU.max))
        bar()
        on("dve", lambda E, s=s, C=C: E.tensor_tensor(
            out=trscr[:, 0:C], in0=s[:, 0:C], in1=sclb[:, :], op=ALU.mult))
        bar()
        t = None
        for j in range(nj):
            t = on("act", lambda E, tl=tiles[i][j], j=j, mw=mw: E.activation(
                out=tl[:, :], in_=trscr[:, j * mw:(j + 1) * mw], func=AF.Copy))
        bar()
        ready.append(t)
    ctx.pop()
    return WTiles(tiles, mw), ready


def emit_p3(ctx, g, x1T, x2T, sh, xin_guard=None, dbg=None):
    sb, ps, on, dma, need = ctx.sb, ctx.ps, ctx.on, ctx.dma, ctx.need
    TCP = 256
    NCHP = T // TCP
    stg = [sb("p3stg0", [128, 4096], F32), sb("p3stg1", [128, 4096], F32)]
    w1b, w1r = emit_bitlinear_prep(ctx, g, "w_fcT", D, 4096, sh, stg, "p3w1")
    w2b, w2r = emit_bitlinear_prep(ctx, g, "w_projT", 4096, D, sh, stg, "p3w2",
                                   guard=w1r[-2:])
    vTb, r1 = load_bf16_tiles(ctx, "p3vT", g["vT"], D, 16, stg, guard=w2r[-2:])
    fbTb, r2 = load_bf16_tiles(ctx, "p3fbT", g["fbT"], 16, 16, stg, guard=r1[-2:])
    uTb, r3 = load_bf16_tiles(ctx, "p3uT", g["uT"], 16, 4096, stg, guard=r2[-1:])
    gcol = sb("p3gate", [128, 1], F32)
    ldg = dma("sync", gcol[:, :], g["gate"][:, :])
    if dbg is not None:
        need("sync", w1r[0])
        dma("sync", dbg["w1t0"][:, :], w1b.tiles[0][0][:, :])
        need("sync", w2r[0])
        dma("sync", dbg["w2t0"][:, :], w2b.tiles[0][0][:, :])
    ctx.need("pe", w1r[-1], w2r[-1], r1[-1], r2[-1], r3[-1], *sh["ones_ready"])

    ps_stat = ps("p3stat", [33, 512])
    ps_A = ps("p3A", [128, 1024])
    ps_b = ps("p3bc", [128, 1024])
    ps_B = ps("p3B", [128, 1024])
    ps_g = ps("p3g", [16, 512])

    _x1 = [sb(f"p3x1_{i}", [128, TCP], F32) for i in range(8)]
    x1sb = [_x1, _x1]
    sqst = [[stg[s][:, j * TCP:(j + 1) * TCP] for j in range(2)] for s in range(2)]
    _ln = dict(rows=sb("p3rows", [1, 4 * TCP], F32),
               mub=sb("p3mub", [128, 2 * TCP], F32),
               xn=[sb(f"p3xn_{i}", [128, TCP], BF16) for i in range(8)])
    lnbufs = [_ln, _ln]
    g1b = sb("p3g1b", [16, TCP], BF16)
    g2b = sb("p3g2b", [16, TCP], BF16)
    hsb = [stg[0][:, 2048:2048 + TCP], stg[0][:, 2048 + TCP:2048 + 2 * TCP]]
    r2b = [sb(f"p3r2b{m}", [128, TCP], BF16) for m in range(32)]
    ot = [sb(f"p3ot{i}", [128, TCP], F32) for i in range(2)]

    slot_guard = [[], []]
    xn_guard = [[], []]
    out_guard = [[] for _ in range(T // TCP)]
    prev_evb = []
    prev_g_ev = []
    prev_w1_start = []
    sq_acts = [None] * 32
    prev_sq_rd = [None] * 32
    prev_hev = [None, None]
    prev_g3 = [None, None]
    ot_store = [None] * 4
    prev_oev = [None, None]

    for c in range(NCHP):
        s = c % 2
        t0 = c * TCP
        lds = []
        for i in range(8):
            if xin_guard is not None:
                need("sync", xin_guard[c * TCP // TC])
            need("sync", *slot_guard[s])
            lds.append(dma("sync", x1sb[s][i][:, :],
                           x1T[i * 128:(i + 1) * 128, t0:t0 + TCP]))
        lds.append(dma("sync", sh["lag"][:, :], x1T[896:1024, t0:t0 + 128]))
        slot_guard[s] = []
        xnb, xno = emit_ln(ctx, sh, x1sb[s], TCP, sh["ones_col"], sh["ones_row"],
                           ps_stat, ps_b, sqst[s], f"p3c{c}_", out_dtype=BF16,
                           extra_need=lds, psb_guard=prev_evb, bufs=lnbufs[s],
                           xn_guard=xn_guard[s])
        xn_guard[s] = []
        if dbg is not None and c == 0:
            for i in range(8):
                need("sync", xno[i])
                dma("sync", dbg["xn0"][i * 128:(i + 1) * 128, :], xnb[i][:, :])
        gmm = None
        for k in range(8):
            need("pe", xno[k], *prev_g_ev)
            gmm = on("pe", lambda E, k=k, xnb=xnb: E.matmul(
                ps_g[0:16, 0:TCP], vTb.full(k), xnb[k][:, :],
                start=(k == 0), stop=(k == 7)))
        need("act", gmm)
        gev = on("act", lambda E: E.activation(out=g1b[:, :], in_=ps_g[0:16, 0:TCP],
                                               func=AF.Copy))
        need("pe", gev)
        gmm2 = on("pe", lambda E: E.matmul(ps_g[0:16, 0:TCP], fbTb.full(0),
                                           g1b[:, :], start=True, stop=True))
        need("act", gmm2)
        gev2 = on("act", lambda E: E.activation(out=g2b[:, :], in_=ps_g[0:16, 0:TCP],
                                                func=AF.Copy))
        prev_g_ev = [gev2]
        for m in range(32):
            b = m % 2
            w1mm = None
            for k in range(8):
                need("pe", xno[k], *prev_w1_start)
                if prev_hev[b] is not None:
                    need("pe", prev_hev[b])
                w1mm = on("pe", lambda E, m=m, k=k, b=b, xnb=xnb: E.matmul(
                    ps_A[:, b * 512:b * 512 + TCP],
                    w1b.mm(k, m), xnb[k][:, :],
                    start=(k == 0), stop=(k == 7)))
            need("pe", gev2)
            if prev_g3[b] is not None:
                need("pe", prev_g3[b])
            g3mm = on("pe", lambda E, m=m, b=b: E.matmul(
                ps_B[:, b * 512:b * 512 + TCP], uTb.mm(0, m),
                g2b[:, :], start=True, stop=True))
            need("act", w1mm, g3mm)
            if prev_sq_rd[m] is not None:
                need("act", prev_sq_rd[m])
            hev = on("act", lambda E, b=b, h=hsb[b]: E.activation(
                out=h[:, :], in_=ps_A[:, b * 512:b * 512 + TCP], func=AF.Copy))
            need("dve", hev, g3mm, ldg)
            st1 = on("dve", lambda E, b=b, h=hsb[b]: E.scalar_tensor_tensor(
                out=h[:, :], in0=ps_B[:, b * 512:b * 512 + TCP], scalar=gcol[:, 0:1],
                in1=h[:, :], op0=ALU.mult, op1=ALU.add))
            prev_g3[b] = st1
            rl = on("dve", lambda E, b=b, h=hsb[b]: E.tensor_scalar(
                out=h[:, :], in0=h[:, :], scalar1=0.0, scalar2=None, op0=ALU.max))
            need("act", rl)
            sq = on("act", lambda E, b=b, h=hsb[b], m=m: E.activation(
                out=r2b[m][:, :], in_=h[:, :], func=AF.Square))
            prev_hev[b] = sq
            sq_acts[m] = sq
            if dbg is not None and c == 0 and m == 0:
                need("sync", hev)
                dma("sync", dbg["h_w1"][:, :], hsb[0][:, :])
                need("sync", st1)
                dma("sync", dbg["h_full"][:, :], hsb[0][:, :])
                need("sync", sq)
                dma("sync", dbg["r2_0"][:, :], r2b[0][:, :])
                dma("sync", dbg["g2b"][:, :], g2b[:, :])
                dma("sync", dbg["g1b"][:, :], g1b[:, :])
        prev_w1_start = []
        for m2 in range(8):
            b = m2 % 2
            w2mm = None
            for k2 in range(32):
                need("pe", sq_acts[k2])
                if k2 == 0 and prev_oev[b] is not None:
                    need("pe", prev_oev[b])
                w2mm = on("pe", lambda E, m2=m2, k2=k2, b=b: E.matmul(
                    ps_A[:, b * 512:b * 512 + TCP],
                    w2b.mm(k2, m2), r2b[k2][:, :],
                    start=(k2 == 0), stop=(k2 == 31)))
            o = ot[m2 % 2]
            need("dve", w2mm, ot_store[m2 % 2])
            oev = on("dve", lambda E, o=o, b=b, x1t=x1sb[s][m2]: E.tensor_tensor(
                out=o[:, :], in0=ps_A[:, b * 512:b * 512 + TCP], in1=x1t[:, :],
                op=ALU.add))
            prev_oev[b] = oev
            need("sync", oev)
            st = dma("sync", x2T[m2 * 128:(m2 + 1) * 128, t0:t0 + TCP], o[:, :])
            ot_store[m2 % 2] = st
            out_guard[c].append(st)
            slot_guard[s].append(oev)
        prev_w1_start = [prev_oev[0], prev_oev[1]]
        prev_evb = xno
        prev_sq_rd = [w2mm] * 32
        prev_hev = [prev_oev[0], prev_oev[1]]
    return out_guard


def build_p3_test(debug=False):
    nc = bass.Bass("TRN2", target_bir_lowering=False, debug=False)
    g = {}
    for k in ("w_fcT", "w_projT", "vT", "fbT", "uT", "gate"):
        g[k] = nc.dram_tensor(k, list(WEIGHT_SHAPES[k]), F32, kind="ExternalInput").ap()
    x1T = nc.dram_tensor("x1T", [D, T], F32, kind="ExternalInput").ap()
    x2T = nc.dram_tensor("x2T", [D, T], F32, kind="ExternalOutput").ap()
    dbg = None
    if debug:
        dbg = dict(
            w1t0=nc.dram_tensor("dbg_w1t0", [128, 256], BF16, kind="ExternalOutput").ap(),
            w2t0=nc.dram_tensor("dbg_w2t0", [128, 256], BF16, kind="ExternalOutput").ap(),
            xn0=nc.dram_tensor("dbg_xn0", [D, 256], BF16, kind="ExternalOutput").ap(),
            r2_0=nc.dram_tensor("dbg_r2_0", [128, 256], BF16, kind="ExternalOutput").ap(),
            g2b=nc.dram_tensor("dbg_g2b", [16, 256], BF16, kind="ExternalOutput").ap(),
            g1b=nc.dram_tensor("dbg_g1b", [16, 256], BF16, kind="ExternalOutput").ap(),
            h_w1=nc.dram_tensor("dbg_h_w1", [128, 256], F32, kind="ExternalOutput").ap(),
            h_full=nc.dram_tensor("dbg_h_full", [128, 256], F32, kind="ExternalOutput").ap(),
        )
    ctx = Ctx(nc)
    sh = emit_shared(ctx)
    og = emit_p3(ctx, g, x1T, x2T, sh, dbg=dbg)
    ctx.need("sync", *og[-1])
    ctx.build()
    return nc


def emit_p4(ctx, g, x3T, x4T, sh, xin_guard=None, dbg=None):
    sb, ps, on, dma, need = ctx.sb, ctx.ps, ctx.on, ctx.dma, ctx.need
    bar = ctx.full_barrier
    NH = 16
    HD = 64
    stg = [sb("p4stg0", [128, 3072], F32), sb("p4stg1", [128, 3072], F32)]
    wqkv, wr = load_bf16_tiles(ctx, "p4qkv", g["w_qkv"], D, 3072, stg)
    bqc = sb("p4bq", [128, 24], F32)
    ldb = dma("sync", bqc[:, :], g["bqkv_col"][:, :])
    eyef = stg[0][:, 2560:2688]
    need("sync", wr[-1], wr[-2])
    lde = dma("sync", eyef[:, :], g["eye128"][:, :])
    eyeb = stg[0][:, 2688:2752].bitcast(BF16)
    need("act", lde)
    on("act", lambda E: E.activation(out=eyeb[:, :], in_=eyef[:, :], func=AF.Copy))
    selz = stg[1][0:16, 1536:2560]
    ones_bcol = sb("p4onesb", [128, 1], BF16)
    on("act", lambda E: E.activation(out=ones_bcol[:, :], in_=sh["ones_col"][:, :],
                                     func=AF.Copy))
    ctx.need("pe", wr[-1], ldb, *sh["ones_ready"])

    x3sb = [sb(f"p4x3_{i}", [128, TC], F32) for i in range(8)]
    sqst = [stg[0][:, 0:TC], stg[0][:, TC:2 * TC]]
    lnb = dict(rows=sb("p4rows", [1, 4 * TC], F32), mub=sb("p4mub", [128, 2 * TC], F32),
               xn=[sb(f"p4xn{i}", [128, TC], BF16) for i in range(8)])
    phit = [stg[0][:, 2 * TC:3 * TC], stg[0][:, 3 * TC:4 * TC]]
    phim = stg[0][:, 4 * TC:5 * TC]
    pq2 = [[sb(f"p4pq_{h}_{c}", [64, TC], BF16) for c in range(NCH)] for h in range(16)]
    pkb = [[sb(f"p4pk_{dt}_{st}", [128, 128], BF16) for st in range(4)] for dt in range(8)]
    vb = [[sb(f"p4v_{dt}_{st}", [128, 128], BF16) for st in range(4)] for dt in range(8)]
    pkT = [[sb(f"p4pkT_{st}_{j}", [128, 256], BF16) for j in range(4)] for st in range(4)]
    vT = [[sb(f"p4vT_{st}_{j}", [128, 256], BF16) for j in range(4)] for st in range(4)]
    kvsb = stg[1][0:64, 3 * TC:3 * TC + 1024]
    ksum_acc = [stg[1][0:1, 2560:3072], stg[1][64:65, 2560:3072]]
    kscol = stg[0][:, 2752:2760]
    need("dve", wr[-1], wr[-2])
    zk = on("dve", lambda E: E.memset(kvsb[:, :], 0.0))

    ctx.push()
    ps_stat = ps("p4stat", [33, 512])
    ps_b = ps("p4bc", [128, 1024])
    ps_qk = ps("p4qk", [128, 1024])
    ps_tr = ps("p4tr", [128, 256], BF16)
    ps_kv = ps("p4kv", [128, 1024])

    out_guard = [[] for _ in range(NCH)]

    for c in range(NCH):
        t0 = c * TC
        bar()
        lds = []
        for i in range(8):
            if xin_guard is not None:
                need("sync", xin_guard[c])
            lds.append(dma("sync", x3sb[i][:, :], x3T[i * 128:(i + 1) * 128, t0:t0 + TC]))
        lds.append(dma("sync", sh["lag"][:, :], x3T[896:1024, t0:t0 + 128]))
        xnb, xno = emit_ln(ctx, sh, x3sb, TC, sh["ones_col"], sh["ones_row"],
                           ps_stat, ps_b, sqst, f"p4c{c}_", out_dtype=BF16,
                           extra_need=lds, bufs=lnb)
        bar()
        prev_ev0 = [None, None]
        prev_tvcast = [None, None]
        for m in range(24):
            b = m % 2
            for k in range(8):
                if k == 0:
                    need("pe", prev_ev0[b])
                mmq = on("pe", lambda E, m=m, k=k, b=b, xnb=xnb: E.matmul(
                    ps_qk[:, b * 512:b * 512 + TC], wqkv.mm(k, m), xnb[k][:, :],
                    start=(k == 0), stop=(k == 7)))
            need("dve", mmq, prev_tvcast[b])
            tv = phit[b]
            ev0 = on("dve", lambda E, b=b, m=m, tv=tv: E.tensor_scalar(
                out=tv[:, :], in0=ps_qk[:, b * 512:b * 512 + TC],
                scalar1=col(bqc, m), scalar2=None, op0=ALU.add))
            prev_ev0[b] = ev0
            if m < 16:
                on("dve", lambda E, tv=tv: E.tensor_scalar(
                    out=phim[:, :], in0=tv[:, :], scalar1=0.0, scalar2=None,
                    op0=ALU.min))
                d1 = on("dve", lambda E: E.engine_nop())
                need("act", d1)
                a1 = on("act", lambda E: E.activation(out=phim[:, :], in_=phim[:, :],
                                                      func=AF.Exp))
                need("dve", a1)
                st1 = on("dve", lambda E, tv=tv: E.scalar_tensor_tensor(
                    out=tv[:, :], in0=tv[:, :], scalar=0.0, in1=phim[:, :],
                    op0=ALU.max, op1=ALU.add))
                dt = m % 8
                if m < 8:
                    need("act", st1)
                    for hh in range(2):
                        cq = on("act", lambda E, dt=dt, c=c, hh=hh, tv=tv: E.activation(
                            out=pq2[2 * dt + hh][c][:, :],
                            in_=tv[hh * 64:(hh + 1) * 64, :], func=AF.Copy))
                    prev_tvcast[b] = cq
                else:
                    need("act", st1)
                    for st in range(4):
                        ck = on("act", lambda E, dt=dt, st=st, tv=tv: E.activation(
                            out=pkb[dt][st][:, :], in_=tv[:, st * 128:(st + 1) * 128],
                            func=AF.Copy))
                    prev_tvcast[b] = ck

            else:
                dt = m - 16
                need("act", ev0)
                for st in range(4):
                    cv = on("act", lambda E, dt=dt, st=st, tv=tv: E.activation(
                        out=vb[dt][st][:, :], in_=tv[:, st * 128:(st + 1) * 128],
                        func=AF.Copy))
                prev_tvcast[b] = cv
        bar()
        tr_t = [ps_tr, ps_qk.bitcast(BF16)]
        prev_evt = [None, None]
        tr_idx = 0
        for st in range(4):
            for dt in range(8):
                for src_t, dst in ((pkb, pkT), (vb, vT)):
                    pt = tr_t[tr_idx % 2]
                    need("pe", prev_evt[tr_idx % 2])
                    tr = on("pe", lambda E, st=st, dt=dt, src_t=src_t, pt=pt: E.transpose(
                        pt[:, 0:128], src_t[dt][st][:, :], eyeb[:, :]))
                    need("act", tr)
                    j, off = divmod(dt * 128, 256)
                    evt = on("act", lambda E, st=st, dst=dst, j=j, off=off, pt=pt: E.activation(
                        out=dst[st][j][:, off:off + 128], in_=pt[:, 0:128],
                        func=AF.Copy))
                    prev_evt[tr_idx % 2] = evt
                    tr_idx += 1
        bar()
        prev_kev = [None, None]
        for h in range(NH):
            b = h % 2
            if prev_kev[b] is not None:
                need("pe", prev_kev[b])
            jh, offh = divmod(h * 64, 256)
            for st in range(4):
                kmm = on("pe", lambda E, h=h, st=st, b=b, jh=jh, offh=offh: E.matmul(
                    ps_kv[0:64, b * 512:b * 512 + 64],
                    pkT[st][jh][:, offh:offh + 64], vT[st][jh][:, offh:offh + 64],
                    start=(st == 0), stop=(st == 3)))
            need("dve", kmm)
            kev = on("dve", lambda E, h=h, b=b: E.tensor_tensor(
                out=kvsb[:, h * 64:(h + 1) * 64], in0=kvsb[:, h * 64:(h + 1) * 64],
                in1=ps_kv[0:64, b * 512:b * 512 + 64], op=ALU.add))
            prev_kev[b] = kev
        bar()
        for j in range(4):
            for st in range(4):
                ksmm = on("pe", lambda E, j=j, st=st: E.matmul(
                    ps_kv[64:65, j * 256:(j + 1) * 256], ones_bcol[:, :],
                    pkT[st][j][:, :], start=(st == 0), stop=(st == 3)))
        bar()
        for r in range(2):
            if c == 0:
                on("act", lambda E, r=r: E.activation(
                    out=ksum_acc[r], in_=ps_kv[64:65, r * 512:(r + 1) * 512],
                    func=AF.Copy))
            else:
                on("dve", lambda E, r=r: E.tensor_tensor(
                    out=ksum_acc[r], in0=ksum_acc[r],
                    in1=ps_kv[64:65, r * 512:(r + 1) * 512], op=ALU.add))
        bar()

    bar()
    if dbg is not None:
        dma("sync", dbg["kvsb"][:, :], kvsb[:, :])
        bar()
    ctx.pop()
    kvTb = [stg[0][0:64, j * 128:(j + 1) * 128].bitcast(BF16) for j in range(4)]
    for j in range(4):
        on("act", lambda E, j=j: E.activation(out=kvTb[j][0:64, :],
                                              in_=kvsb[:, j * 256:(j + 1) * 256],
                                              func=AF.Copy))
    bar()
    lsz = dma("sync", selz[:, :], g["selz"][:, :])
    ksE = [stg[0][0:64, 512 + h * 8:512 + h * 8 + 8].bitcast(BF16) for h in range(16)]
    kss_d = ctx.nc.dram_tensor("p4kss", [2, 512], F32, kind="Internal").ap()
    dma("sync", kss_d[0:1, :], ksum_acc[0])
    dma("sync", kss_d[1:2, :], ksum_acc[1])
    bar()
    dma("sync", kscol[:, :], kss_d.rearrange("r (a p) -> p (r a)", p=128), allow_slow_non_contiguous=True)
    bar()
    for h in range(16):
        on("dve", lambda E, h=h: E.memset(ksE[h][:, :], 0.0))
    bar()
    for h in range(16):
        dt, hh = divmod(h, 2)
        on("act", lambda E, h=h, dt=dt, hh=hh: E.activation(
            out=ksE[h][0:64, h:h + 1],
            in_=kscol[hh * 64:(hh + 1) * 64, dt:dt + 1], func=AF.Copy))
    bar()
    zsb = sb("p4zsb", [16, TC], F32)
    zb = stg[1][:, 0:TC]
    att = [stg[1][:, TC:2 * TC], stg[1][:, 2 * TC:3 * TC]]
    ps_z = ps("p4z", [16, 512])
    ps_at = ps("p4at", [128, 1024])
    ps_zb = ps("p4zbb", [128, 512])
    for c in range(NCH):
        t0 = c * TC
        for h in range(16):
            zmm = on("pe", lambda E, c=c, h=h: E.matmul(
                ps_z[0:16, 0:TC], ksE[h][:, :], pq2[h][c][:, :],
                start=(h == 0), stop=(h == 15)))
        need("dve", zmm)
        on("dve", lambda E: E.tensor_scalar(
            out=zsb[:, 0:TC], in0=ps_z[0:16, 0:TC],
            scalar1=col(sh["cst"], 2)[0:16, :], scalar2=None, op0=ALU.add))
        zr = on("dve", lambda E: E.reciprocal(
            out=zsb[:, 0:TC], in_=zsb[:, 0:TC]))
        bar()
        lds = []
        for i in range(8):
            lds.append(dma("sync", x3sb[i][:, :], x3T[i * 128:(i + 1) * 128, t0:t0 + TC]))
        lds.append(dma("sync", sh["lag"][:, :], x3T[896:1024, t0:t0 + 128]))
        bar()
        for hp in range(8):
            b = hp % 2
            for hh in range(2):
                h = hp * 2 + hh
                j, off = divmod(h * 64, 256)
                amm = on("pe", lambda E, h=h, hh=hh, b=b, c=c, j=j, off=off: E.matmul(
                    ps_at[hh * 64:hh * 64 + 64, b * 512:b * 512 + TC],
                    kvTb[j][:, off:off + 64], pq2[h][c][:, :],
                    start=True, stop=True))
            need("pe", zr, lsz)
            zbm = on("pe", lambda E, hp=hp: E.matmul(
                ps_zb[:, 0:TC], selz[:, hp * 128:(hp + 1) * 128], zsb[:, 0:TC],
                start=True, stop=True))
            need("act", zbm)
            zbe = on("act", lambda E: E.activation(out=zb[:, :], in_=ps_zb[:, 0:TC],
                                                   func=AF.Copy))
            need("dve", amm, zbe, *lds)
            a1 = on("dve", lambda E, b=b, hp=hp: E.tensor_tensor(
                out=att[b][:, :], in0=ps_at[:, b * 512:b * 512 + TC], in1=zb[:, :],
                op=ALU.mult))
            a2 = on("dve", lambda E, b=b, hp=hp: E.tensor_tensor(
                out=att[b][:, :], in0=att[b][:, :], in1=x3sb[hp][:, :], op=ALU.add))
            need("sync", a2)
            stx = dma("sync", x4T[hp * 128:(hp + 1) * 128, t0:t0 + TC], att[b][:, :])
            out_guard[c].append(stx)
            bar()
    return out_guard


def build_p4_test():
    nc = bass.Bass("TRN2", target_bir_lowering=False, debug=False)
    g = {}
    for k in ("w_qkv", "bqkv_col", "selz", "eye128"):
        g[k] = nc.dram_tensor(k, list(WEIGHT_SHAPES[k]), F32, kind="ExternalInput").ap()
    x3T = nc.dram_tensor("x3T", [D, T], F32, kind="ExternalInput").ap()
    x4T = nc.dram_tensor("x4T", [D, T], F32, kind="ExternalOutput").ap()
    dbg = dict(kvsb=nc.dram_tensor("dbg_kvsb", [64, 1024], F32, kind="ExternalOutput").ap(),
               ksum=nc.dram_tensor("dbg_ksum", [128, 8], F32, kind="ExternalOutput").ap())
    ctx = Ctx(nc)
    sh = emit_shared(ctx)
    og = emit_p4(ctx, g, x3T, x4T, sh, dbg=dbg)
    ctx.need("sync", *og[-1])
    ctx.build()
    return nc


def emit_p1(ctx, g, xT, scr, sh):
    sb, ps, on, dma, need = ctx.sb, ctx.ps, ctx.on, ctx.dma, ctx.need
    bar = ctx.full_barrier
    stg = [sb("p1stg0", [128, 4096], F32), sb("p1stg1", [128, 4096], F32)]
    win, wr1 = load_bf16_tiles(ctx, "p1win", g["w_inT"], D, 4096, stg)
    wxp, wr2 = load_bf16_tiles(ctx, "p1wxp", g["w_xprojT"], DI, 33, stg, guard=wr1[-2:])
    wdt, wr3 = load_bf16_tiles(ctx, "p1wdt", g["w_dtT"], 33, DI, stg, guard=wr2[-2:])
    cwc = sb("p1cw", [128, 64], F32)
    cbc = sb("p1cb", [128, 16], F32)
    bdtc = sb("p1bdt", [128, 16], F32)
    lngc = sb("p1lng", [128, 8], F32)
    lnbc = sb("p1lnb", [128, 8], F32)
    lcw = dma("sync", cwc[:, :], g["convw_col"][:, :])
    dma("sync", cbc[:, :], g["convb_col"][:, :])
    dma("sync", bdtc[:, :], g["bdt_col"][:, :])
    dma("sync", lngc[:, :], g["lng_col"][:, :])
    lcl = dma("sync", lnbc[:, :], g["lnb_col"][:, :])
    ctx.need("pe", wr3[-1], lcl, *sh["ones_ready"])
    ctx.need("dve", wr3[-1], lcl)
    ctx.need("act", wr3[-1], lcl)

    ps_stat = ps("p1stat", [33, 512])
    ps_b = ps("p1bc", [128, 1024])
    ps_A = ps("p1A", [128, 1024])
    ps_xp = ps("p1xp", [33, 512])
    ps_dt = ps("p1dt", [128, 1024])

    xsb = [sb(f"p1x_{i}", [128, TC], F32) for i in range(8)]
    sqst = [stg[0][:, 0:TC], stg[0][:, TC:2 * TC]]
    lnb = dict(rows=sb("p1rows", [1, 5 * TC], F32), mub=sb("p1mub", [128, 2 * TC], F32),
               xn=[sb(f"p1xn{i}", [128, TC], BF16) for i in range(8)])
    xsm = [sb(f"p1xsm{cb}", [128, TC + 3], F32) for cb in range(16)]
    xsp = [sb(f"p1xsp{cb}", [128, TC], F32) for cb in range(16)]
    bounce = [stg[0][:, 2 * TC:3 * TC], stg[0][:, 3 * TC:4 * TC]]
    convt = [stg[0][:, 4 * TC:5 * TC], stg[0][:, 5 * TC:6 * TC]]
    xpf = stg[1][0:33, 1536:1536 + TC]
    xpb = stg[1][0:33, 2048:2048 + TC // 2].bitcast(BF16)
    xspb = [stg[1][:, 0:TC].bitcast(BF16)[:, 0:TC],
            stg[1][:, TC // 2:TC // 2 + TC].bitcast(BF16)[:, 0:TC]]
    etile = [stg[1][:, TC:2 * TC], stg[1][:, 2 * TC:3 * TC]]

    for cb in range(16):
        on("dve", lambda E, cb=cb: E.memset(xsm[cb][:, 0:3], 0.0))
    bar()
    prev_wev = [None, None]
    prev_cs = [None, None]
    prev_xmm = [None, None]
    prev_xev = None
    prev_e2 = [None, None]
    bq_st = [None, None]
    xsp_st = [None] * 16
    tail_cp = [None] * 16
    bc_st = [None, None]
    l_st = [None, None]

    for c in range(NCH):
        t0 = c * TC
        lds = []
        for i in range(8):
            lds.append(dma("sync", xsb[i][:, :], xT[i * 128:(i + 1) * 128, t0:t0 + TC]))
        lds.append(dma("sync", sh["lag"][:, :], xT[896:1024, t0:t0 + 128]))
        xnb, xno = emit_ln(ctx, sh, xsb, TC, sh["ones_col"], sh["ones_row"],
                           ps_stat, ps_b, sqst, f"p1c{c}_", out_dtype=BF16,
                           extra_need=lds, bufs=lnb, gamma=lngc, beta=lnbc,
                           combined=True)
        for m in range(32):
            b = m % 2
            for k in range(8):
                need("pe", xno[k])
                if k == 0 and prev_wev[b] is not None:
                    need("pe", prev_wev[b])
                wmm = on("pe", lambda E, m=m, k=k, b=b, xnb=xnb: E.matmul(
                    ps_A[:, b * 512:b * 512 + TC], win.mm(k, m), xnb[k][:, :],
                    start=(k == 0), stop=(k == 7)))
            need("act", wmm)
            if m < 16:
                ev = on("act", lambda E, m=m, b=b: E.activation(
                    out=xsm[m][:, 3:3 + TC], in_=ps_A[:, b * 512:b * 512 + TC],
                    func=AF.Copy))
            else:
                bo = bounce[b]
                need("act", bq_st[b])
                ev = on("act", lambda E, m=m, b=b, bo=bo: E.activation(
                    out=bo[:, :], in_=ps_A[:, b * 512:b * 512 + TC], func=AF.Silu))
                need("sync", ev)
                bq_st[b] = dma("sync",
                               scr["sz"][(m - 16) * 128:(m - 15) * 128, t0:t0 + TC],
                               bo[:, :])
            prev_wev[b] = ev
        for cb in range(16):
            b = cb % 2
            tmp = convt[b]
            need("dve", prev_wev[0], prev_wev[1], prev_cs[b])
            on("dve", lambda E, cb=cb, tmp=tmp: E.tensor_scalar(
                out=tmp[:, :], in0=xsm[cb][:, 0:TC], scalar1=col(cwc, cb * 4),
                scalar2=None, op0=ALU.mult))
            dvl = None
            for k in range(1, 4):
                dvl = on("dve", lambda E, cb=cb, k=k, tmp=tmp: E.scalar_tensor_tensor(
                    out=tmp[:, :], in0=xsm[cb][:, k:k + TC], scalar=col(cwc, cb * 4 + k),
                    in1=tmp[:, :], op0=ALU.mult, op1=ALU.add))
            need("act", dvl, xsp_st[cb])
            cs = on("act", lambda E, cb=cb, tmp=tmp: E.activation(
                out=xsp[cb][:, :], in_=tmp[:, :], func=AF.Silu, bias=col(cbc, cb)))
            prev_cs[b] = cs
            need("sync", cs)
            xsp_st[cb] = dma("sync", scr["xsp"][cb * 128:(cb + 1) * 128, t0:t0 + TC],
                             xsp[cb][:, :])
            tl = on("dve", lambda E, cb=cb: E.tensor_copy(
                out=xsm[cb][:, 0:3], in_=xsm[cb][:, TC:TC + 3]))
            tail_cp[cb] = tl
        prev_wev = [tail_cp[15], tail_cp[15]]
        for cb in range(16):
            b = cb % 2
            need("act", prev_cs[0], prev_cs[1])
            if prev_xmm[b] is not None:
                need("act", prev_xmm[b])
            cst = on("act", lambda E, cb=cb, b=b: E.activation(
                out=xspb[b][:, :], in_=xsp[cb][:, :], func=AF.Copy))
            need("pe", cst)
            if cb == 0:
                need("pe", prev_xev)
            xmm = on("pe", lambda E, cb=cb, b=b: E.matmul(
                ps_xp[0:33, 0:TC], wxp.full(cb), xspb[b][:, :],
                start=(cb == 0), stop=(cb == 15)))
            prev_xmm[b] = xmm
        need("act", xmm, bc_st[0], bc_st[1])
        xev = on("act", lambda E: E.activation(out=xpf[:, :], in_=ps_xp[0:33, 0:TC],
                                               func=AF.Copy))
        prev_xev = xev
        need("sync", xev)
        bc_st[0] = dma("sync", scr["B"][:, t0:t0 + TC], xpf[1:17, :])
        bc_st[1] = dma("sync", scr["C"][:, t0:t0 + TC], xpf[17:33, :])
        xcb = on("act", lambda E: E.activation(out=xpb[:, :], in_=xpf[:, :],
                                               func=AF.Copy))
        for m in range(16):
            b = m % 2
            need("pe", xcb)
            if prev_e2[b] is not None:
                need("pe", prev_e2[b])
            dmm = on("pe", lambda E, m=m, b=b: E.matmul(
                ps_dt[:, b * 512:b * 512 + TC], wdt.mm(0, m), xpb[:, :],
                start=True, stop=True))
            need("act", dmm, l_st[b])
            e1 = on("act", lambda E, m=m, b=b: E.activation(
                out=etile[b][:, :], in_=ps_dt[:, b * 512:b * 512 + TC],
                func=AF.Exp, bias=col(bdtc, m)))
            e2 = on("act", lambda E, b=b: E.activation(
                out=etile[b][:, :], in_=etile[b][:, :], func=AF.Ln,
                bias=col(sh["cst"], 1)))
            prev_e2[b] = e2
            need("sync", e2)
            l_st[b] = dma("sync", scr["l"][m * 128:(m + 1) * 128, t0:t0 + TC],
                          etile[b][:, :])
    return None


TS2 = 32
P2BARS = set()


def emit_p2(ctx, g, xT, scr, x1T, sh):
    sb, ps, on, dma, need = ctx.sb, ctx.ps, ctx.on, ctx.dma, ctx.need
    bar = ctx.full_barrier
    stg = [sb("p2stg0", [128, 1024], F32), sb("p2stg1", [128, 1024], F32)]
    wout, wro = load_bf16_tiles(ctx, "p2wout", g["w_outT"], DI, D, stg)
    alog = sb("p2alog", [128, 256], F32)
    A2 = sb("p2A2", [128, 256], F32)
    R2n = sb("p2R2n", [128, 256], F32)
    dcol = sb("p2dcol", [128, 16], F32)
    la = dma("sync", alog[:, :], g["alog_col"][:, :])
    ld = dma("sync", dcol[:, :], g["d_col"][:, :])
    need("act", la)
    ae = on("act", lambda E: E.activation(out=A2[:, :], in_=alog[:, :], func=AF.Exp))
    need("dve", ae, ld)
    on("dve", lambda E: E.tensor_scalar(out=A2[:, :], in0=A2[:, :], scalar1=-1.0,
                                        scalar2=None, op0=ALU.mult))
    on("dve", lambda E: E.tensor_scalar(out=R2n[:, :], in0=A2[:, :], scalar1=1e-8,
                                        scalar2=None, op0=ALU.add))
    on("dve", lambda E: E.reciprocal(out=R2n[:, :], in_=R2n[:, :]))
    on("dve", lambda E: E.tensor_scalar(out=R2n[:, :], in0=R2n[:, :], scalar1=-1.0,
                                        scalar2=None, op0=ALU.mult))
    ctx.need("pe", wro[-1], *sh["ones_ready"])
    bar()

    NSC = T // TS2
    dA = sb("p2dA", [128, 16 * 16 * TS2], F32)
    dBx = sb("p2dBx", [128, 16 * 16 * TS2], F32)
    hh = sb("p2hh", [128, 256 * TS2], F32)
    hprev = sb("p2hprev", [128, 256], F32)
    lch = sb("p2lch", [128, 16 * TS2], F32)
    xch = sb("p2xch", [128, 16 * TS2], F32)
    szch = sb("p2szch", [128, 16 * TS2], F32)
    Brow = sb("p2Brow", [1, 16 * TS2], F32)
    Crow = sb("p2Crow", [1, 16 * TS2], F32)
    Bb = sb("p2Bb", [128, 16 * TS2], F32)
    Cb = sb("p2Cb", [128, 16 * TS2], F32)
    ytile = sb("p2y", [128, 16 * TS2], F32)
    yzq = [sb(f"p2yzq{cb}", [128, 128], BF16) for cb in range(16)]
    xin = [sb(f"p2xin{i}", [128, 128], F32) for i in range(8)]
    ot = [sb(f"p2ot{i}", [128, 128], F32) for i in range(2)]
    ps_bc = ps("p2bc", [128, 1024])
    ps_o = ps("p2o", [128, 1024])

    ydx = sb("p2ydx", [128, 16 * TS2], F32)
    on("dve", lambda E: E.memset(hprev[:, :], 0.0))
    bar()
    chunk_war = []
    prev_bc_ev = []
    prev_cst = None
    prev_omm = None
    prev_out_ev = [None, None]
    prev_oev_all = []
    ot_st = [None, None]
    yzq_casts = []

    for sc in range(NSC):
        t0 = sc * TS2
        need("sync", *chunk_war)
        l1 = dma("sync", lch[:, :].rearrange("p (cb t) -> p cb t", cb=16),
                 scr["l"][:, t0:t0 + TS2].rearrange("(cb p) t -> p cb t", p=128))
        l2 = dma("sync", xch[:, :].rearrange("p (cb t) -> p cb t", cb=16),
                 scr["xsp"][:, t0:t0 + TS2].rearrange("(cb p) t -> p cb t", p=128))
        l3 = dma("sync", szch[:, :].rearrange("p (cb t) -> p cb t", cb=16),
                 scr["sz"][:, t0:t0 + TS2].rearrange("(cb p) t -> p cb t", p=128))
        l4 = dma("sync", Brow[:, :].rearrange("o (n t) -> o n t", n=16),
                 scr["B"][:, t0:t0 + TS2].unsqueeze(0))
        l5 = dma("sync", Crow[:, :].rearrange("o (n t) -> o n t", n=16),
                 scr["C"][:, t0:t0 + TS2].unsqueeze(0))
        lagd = dma("sync", sh["lag"][0:16, 0:TS2], scr["C"][:, t0:t0 + TS2])
        if "a" in P2BARS:
            bar()
        bcev = []
        for j, (row, ldx, dst) in enumerate(((Brow, l4, Bb), (Crow, l5, Cb))):
            need("pe", ldx, lagd, *prev_bc_ev)
            bm = on("pe", lambda E, row=row, j=j: E.matmul(
                ps_bc[:, j * 512:j * 512 + 16 * TS2], sh["ones_row"][:, :], row[0:1, :],
                start=True, stop=True))
            need("act", bm)
            bcev.append(on("act", lambda E, dst=dst, j=j: E.activation(
                out=dst[:, :], in_=ps_bc[:, j * 512:j * 512 + 16 * TS2], func=AF.Copy)))
        prev_bc_ev = bcev
        if "b" in P2BARS:
            bar()
        d4 = lambda slab: slab[:, :].rearrange("p (cb n t) -> p cb n t", cb=16, n=16)
        lb = lambda buf: buf[:, :].rearrange("p (cb t) -> p cb t", cb=16) \
            .unsqueeze(2).broadcast_to([128, 16, 16, TS2])
        cb4 = lambda buf: buf[:, :].rearrange("p (cb n) -> p cb n", cb=16) \
            .unsqueeze(3).broadcast_to([128, 16, 16, TS2])
        need("dve", l1, l2, lagd)
        for cb in range(16):
            stt1 = on("dve", lambda E, cb=cb: E.scalar_tensor_tensor(
                out=dA[:, cb * 16 * TS2:(cb + 1) * 16 * TS2].rearrange(
                    "p (n t) -> p n t", n=16),
                in0=lch[:, cb * TS2:(cb + 1) * TS2].unsqueeze(1)
                    .broadcast_to([128, 16, TS2]),
                scalar=0.001, in1=A2[:, cb * 16:(cb + 1) * 16].unsqueeze(2)
                    .broadcast_to([128, 16, TS2]),
                op0=ALU.add, op1=ALU.mult))
        need("act", stt1)
        ex = on("act", lambda E: E.activation(out=dA[:, :], in_=dA[:, :], func=AF.Exp))
        need("dve", ex)
        for cb in range(16):
            on("dve", lambda E, cb=cb: E.scalar_tensor_tensor(
                out=dBx[:, cb * 16 * TS2:(cb + 1) * 16 * TS2].rearrange(
                    "p (n t) -> p n t", n=16),
                in0=dA[:, cb * 16 * TS2:(cb + 1) * 16 * TS2].rearrange(
                    "p (n t) -> p n t", n=16), scalar=-1.0,
                in1=R2n[:, cb * 16:(cb + 1) * 16].unsqueeze(2)
                    .broadcast_to([128, 16, TS2]),
                op0=ALU.add, op1=ALU.mult))
        on("dve", lambda E: E.tensor_tensor(
            out=d4(dBx), in0=d4(dBx), in1=lb(xch), op=ALU.mult))
        need("dve", bcev[0])
        on("dve", lambda E: E.tensor_tensor(
            out=d4(dBx), in0=d4(dBx),
            in1=Bb[:, :].rearrange("p (n t) -> p n t", n=16).unsqueeze(1)
                .broadcast_to([128, 16, 16, TS2]), op=ALU.mult))
        if "c" in P2BARS:
            bar()
        def _st(slab, t):
            return slab[:, :].rearrange("p (cb n t2) -> p cb n t2",
                                        cb=16, n=16, t2=TS2)[:, :, :, t:t + 1]

        def _hv(t):
            return hh[:, t * 256:t * 256 + 256].rearrange(
                "p (cb n) -> p cb n", cb=16).unsqueeze(3)
        for t in range(TS2):
            hsrc = (hprev[:, :].rearrange("p (cb n) -> p cb n", cb=16).unsqueeze(3)
                    if t == 0 else _hv(t - 1))
            on("dve", lambda E, t=t, hsrc=hsrc: E.tensor_tensor(
                out=_hv(t), in0=hsrc, in1=_st(dA, t), op=ALU.mult))
            on("dve", lambda E, t=t: E.tensor_tensor(
                out=_hv(t), in0=_hv(t), in1=_st(dBx, t), op=ALU.add))
            on("dve", lambda E, t=t: E.tensor_scalar(
                out=hh[:, t * 256:t * 256 + 256],
                in0=hh[:, t * 256:t * 256 + 256],
                scalar1=10.0, scalar2=-10.0, op0=ALU.min, op1=ALU.max))
        on("dve", lambda E: E.tensor_copy(
            out=hprev[:, :], in_=hh[:, (TS2 - 1) * 256:TS2 * 256]))
        if "d" in P2BARS:
            bar()
        CbT = Cb[:, :].rearrange("p (n t) -> p t n", n=16)
        ytT = ytile[:, :].rearrange("p (cb t) -> p t cb", cb=16)
        need("dve", bcev[1], prev_cst)
        for t8 in range(TS2 // 8):
            on("dve", lambda E, t8=t8: E.tensor_tensor(
                out=dBx[:, t8 * 2048:(t8 + 1) * 2048].rearrange(
                    "p (t cb n) -> p t cb n", t=8, cb=16),
                in0=hh[:, t8 * 8 * 256:(t8 + 1) * 8 * 256].rearrange(
                    "p (t cb n) -> p t cb n", t=8, cb=16),
                in1=CbT[:, t8 * 8:(t8 + 1) * 8, :].unsqueeze(2)
                    .broadcast_to([128, 8, 16, 16]), op=ALU.mult))
            on("dve", lambda E, t8=t8: E.tensor_reduce(
                out=ytT[:, t8 * 8:(t8 + 1) * 8, :].unsqueeze(3),
                in_=dBx[:, t8 * 2048:(t8 + 1) * 2048].rearrange(
                    "p (t cb n) -> p t cb n", t=8, cb=16),
                axis=AX.X, op=ALU.add))
        on("dve", lambda E: E.tensor_tensor(
            out=ydx[:, :].rearrange("p (cb t) -> p cb t", cb=16),
            in0=xch[:, :].rearrange("p (cb t) -> p cb t", cb=16),
            in1=dcol[:, :].unsqueeze(2).broadcast_to([128, 16, TS2]), op=ALU.mult))
        on("dve", lambda E: E.tensor_tensor(
            out=ytile[:, :], in0=ytile[:, :], in1=ydx[:, :], op=ALU.add))
        need("dve", l3)
        yz = on("dve", lambda E: E.tensor_tensor(
            out=ytile[:, :], in0=ytile[:, :], in1=szch[:, :], op=ALU.mult))
        chunk_war = [yz] + bcev
        need("act", yz)
        q = sc % 4
        cst_last = None
        for cb in range(16):
            if q == 0 and prev_omm is not None:
                need("act", prev_omm)
            cst_last = on("act", lambda E, cb=cb, q=q: E.activation(
                out=yzq[cb][:, q * TS2:(q + 1) * TS2],
                in_=ytile[:, cb * TS2:(cb + 1) * TS2], func=AF.Copy))
        yzq_casts.append(cst_last)
        prev_cst = cst_last
        if "e" in P2BARS:
            bar()
        if q == 3:
            tg0 = (sc - 3) * TS2
            lxs = []
            for i in range(8):
                need("sync", prev_out_ev[0], prev_out_ev[1])
                lxs.append(dma("sync", xin[i][:, :],
                               xT[i * 128:(i + 1) * 128, tg0:tg0 + 128]))
            for m in range(8):
                b = m % 2
                for k in range(16):
                    need("pe", yzq_casts[-1])
                    if k == 0 and prev_out_ev[b] is not None:
                        need("pe", prev_out_ev[b])
                    omm = on("pe", lambda E, m=m, k=k, b=b: E.matmul(
                        ps_o[:, b * 512:b * 512 + 128], wout.mm(k, m), yzq[k][:, :],
                        start=(k == 0), stop=(k == 15)))
                need("dve", omm, lxs[m], ot_st[b], ot_st[1 - b])
                oev = on("dve", lambda E, m=m, b=b: E.tensor_tensor(
                    out=ot[b][:, :], in0=ps_o[:, b * 512:b * 512 + 128],
                    in1=xin[m][:, :], op=ALU.add))
                prev_out_ev[b] = oev
                need("sync", oev)
                st = dma("sync", x1T[m * 128:(m + 1) * 128, tg0:tg0 + 128], ot[b][:, :])
                ot_st[b] = st
                prev_oev_all.append(st)
            prev_omm = omm
            yzq_casts = []
    return None


def build_p12_test():
    nc = bass.Bass("TRN2", target_bir_lowering=False, debug=False)
    g = {}
    for k in ("w_inT", "convw_col", "convb_col", "alog_col", "d_col", "w_xprojT",
              "w_dtT", "bdt_col", "w_outT", "lng_col", "lnb_col"):
        g[k] = nc.dram_tensor(k, list(WEIGHT_SHAPES[k]), F32, kind="ExternalInput").ap()
    xT = nc.dram_tensor("xT", [D, T], F32, kind="ExternalInput").ap()
    x1T = nc.dram_tensor("x1T", [D, T], F32, kind="ExternalOutput").ap()
    scr = dict(
        xsp=nc.dram_tensor("scr_xsp", [DI, T], F32, kind="Internal").ap(),
        sz=nc.dram_tensor("scr_sz", [DI, T], F32, kind="Internal").ap(),
        l=nc.dram_tensor("scr_l", [DI, T], F32, kind="Internal").ap(),
        B=nc.dram_tensor("scr_B", [16, T], F32, kind="Internal").ap(),
        C=nc.dram_tensor("scr_C", [16, T], F32, kind="Internal").ap(),
    )
    ctx = Ctx(nc)
    sh = emit_shared(ctx)
    ctx.push()
    emit_p1(ctx, g, xT, scr, sh)
    ctx.full_barrier()
    ctx.pop()
    emit_p2(ctx, g, xT, scr, x1T, sh)
    ctx.full_barrier()
    ctx.build()
    return nc


_NC_CACHE = {}


def _get_ncs():
    if "ncs" not in _NC_CACHE:
        _NC_CACHE["ncs"] = (build_p12_test(), build_p3_test(), build_p4_test(),
                            build_p5_test())
    return _NC_CACHE["ncs"]


P12_W = ("w_inT", "convw_col", "convb_col", "alog_col", "d_col", "w_xprojT",
         "w_dtT", "bdt_col", "w_outT", "lng_col", "lnb_col")
P3_W = ("w_fcT", "w_projT", "vT", "fbT", "uT", "gate")
P4_W = ("w_qkv", "bqkv_col", "selz", "eye128")
P5_W = ("u1", "v1", "b1_col", "u2", "v2", "b2_col")


def kernel_unfused(**inputs):
    g = host_layouts(inputs)
    x = np.ascontiguousarray(np.asarray(inputs["L0_attn_x"], np.float32))
    nc12, nc3, nc4, nc5 = _get_ncs()
    cores = list(range(NCORE))

    ims = [{k: g[k] for k in P12_W} for _ in cores]
    for b in cores:
        ims[b]["xT"] = np.ascontiguousarray(x[b].T)
    r1 = run_bass_kernel_spmd(nc12, ims, core_ids=cores)
    x1 = [r["x1T"] for r in r1.results]

    ims = [{k: g[k] for k in P3_W} for _ in cores]
    for b in cores:
        ims[b]["x1T"] = np.ascontiguousarray(x1[b])
    r2 = run_bass_kernel_spmd(nc3, ims, core_ids=cores)
    x2 = [r["x2T"] for r in r2.results]

    ims = [{k: g[k] for k in P4_W} for _ in cores]
    for b in cores:
        ims[b]["x3T"] = np.ascontiguousarray(x2[b])
    r3 = run_bass_kernel_spmd(nc4, ims, core_ids=cores)
    x3 = [r["x4T"] for r in r3.results]

    ims = [{k: g[k] for k in P5_W} for _ in cores]
    for b in cores:
        ims[b]["x4T"] = np.ascontiguousarray(x3[b])
    r4 = run_bass_kernel_spmd(nc5, ims, core_ids=cores)
    out = np.stack([np.ascontiguousarray(r["outT"].T) for r in r4.results])
    return out.astype(np.float32)


ALL_W = tuple(set(P12_W) | set(P3_W) | set(P4_W) | set(P5_W))


FUSED_DBG = False


def build_fused():
    nc = bass.Bass("TRN2", target_bir_lowering=False, debug=False)
    g = {}
    for k in ALL_W:
        g[k] = nc.dram_tensor(k, list(WEIGHT_SHAPES[k]), F32, kind="ExternalInput").ap()
    xT = nc.dram_tensor("xT", [D, T], F32, kind="ExternalInput").ap()
    outT = nc.dram_tensor("outT", [D, T], F32, kind="ExternalOutput").ap()
    ikind = "ExternalOutput" if FUSED_DBG else "Internal"
    x1T = nc.dram_tensor("x1T", [D, T], F32, kind=ikind).ap()
    x2T = nc.dram_tensor("x2T", [D, T], F32, kind=ikind).ap()
    x3T = nc.dram_tensor("x3T", [D, T], F32, kind=ikind).ap()
    scr = dict(
        xsp=nc.dram_tensor("scr_xsp", [DI, T], F32, kind="Internal").ap(),
        sz=nc.dram_tensor("scr_sz", [DI, T], F32, kind="Internal").ap(),
        l=nc.dram_tensor("scr_l", [DI, T], F32, kind="Internal").ap(),
        B=nc.dram_tensor("scr_B", [16, T], F32, kind="Internal").ap(),
        C=nc.dram_tensor("scr_C", [16, T], F32, kind="Internal").ap(),
    )
    ctx = Ctx(nc)
    sh = emit_shared(ctx)
    ctx.push()
    emit_p1(ctx, g, xT, scr, sh)
    ctx.full_barrier()
    ctx.pop()
    ctx.push()
    emit_p2(ctx, g, xT, scr, x1T, sh)
    ctx.full_barrier()
    ctx.pop()
    ctx.push()
    emit_p3(ctx, g, x1T, x2T, sh)
    ctx.full_barrier()
    ctx.pop()
    ctx.push()
    emit_p4(ctx, g, x2T, x3T, sh)
    ctx.full_barrier()
    ctx.pop()
    dump_w = None
    if FUSED_DBG:
        dump_w = dict(
            u1=nc.dram_tensor("dw_u1", [128, 8 * 256], BF16, kind="ExternalOutput").ap(),
            v1=nc.dram_tensor("dw_v1", [128, 32 * 256], BF16, kind="ExternalOutput").ap(),
            u2=nc.dram_tensor("dw_u2", [128, 32 * 256], BF16, kind="ExternalOutput").ap(),
            v2=nc.dram_tensor("dw_v2", [128, 8 * 256], BF16, kind="ExternalOutput").ap(),
            b1=nc.dram_tensor("dw_b1", [128, 32], F32, kind="ExternalOutput").ap(),
            b2=nc.dram_tensor("dw_b2", [128, 8], F32, kind="ExternalOutput").ap(),
        )
    og = emit_p5(ctx, g, x3T, outT, sh, dump_w=dump_w)
    ctx.need("sync", *og[-1])
    ctx.full_barrier(engs=("sync",))
    ctx.build()
    return nc


def kernel(**inputs):
    _ = inputs.get("L1_attn_W_proj")
    _ = inputs.get("L1_attn_b_proj")
    g = host_layouts(inputs)
    x = np.ascontiguousarray(np.asarray(inputs["L0_attn_x"], np.float32))
    if "fused" not in _NC_CACHE:
        _NC_CACHE["fused"] = build_fused()
    nc = _NC_CACHE["fused"]
    cores = list(range(NCORE))
    ims = [{k: g[k] for k in ALL_W} for _ in cores]
    for b in cores:
        ims[b]["xT"] = np.ascontiguousarray(x[b].T)
    r = run_bass_kernel_spmd(nc, ims, core_ids=cores)
    out = np.stack([np.ascontiguousarray(q["outT"].T) for q in r.results])
    return out.astype(np.float32)
